# version 52
# baseline (speedup 1.0000x reference)
import numpy as np
import ml_dtypes
import concourse.bass as bass
import concourse.mybir as mybir
from concourse.bass import AP
from concourse.bass_utils import run_bass_kernel_spmd

F32 = mybir.dt.float32
BF16 = mybir.dt.bfloat16
U8 = mybir.dt.uint8
AF = mybir.ActivationFunctionType
ALU = mybir.AluOpType
AX = mybir.AxisListType

NCORES = 8
ELU_PARTS = "aerx"
BS = 2
S = 2048
D = 1024
H = 16
HD = 64
HID = 4096
RK = 16
T = 256
TPS = S // T
NTILE = BS * TPS
KD = D // 128


_STREAMS = ("pe", "dve", "scalar", "gpsimd", "sync")
_SEM_OF = {("pe", "c"): "pe", ("dve", "c"): "dve", ("scalar", "c"): "act",
           ("scalar", "d"): "sdma", ("gpsimd", "c"): "pool",
           ("gpsimd", "d"): "gdma", ("sync", "d"): "sync", ("sync", "c"): "syncc"}


class Buf:
    __slots__ = ("name", "writers", "readers", "exclusive")

    def __init__(self, name, exclusive=False):
        self.name = name
        self.writers = []
        self.readers = []
        self.exclusive = exclusive


class Prog:
    def __init__(self):
        self.ops = []
        self.stream_ops = {s: [] for s in _STREAMS}
        self.drain_op = None

    def add(self, stream, kind, fn, reads=(), writes=(), deps=(), tag=""):
        i = len(self.ops)
        dep = set(deps)
        for b in reads:
            dep.update(b.writers)
            if b.exclusive:
                dep.update(b.readers)
        for b in writes:
            dep.update(b.writers)
            dep.update(b.readers)
        dep.discard(i)
        op = dict(id=i, stream=stream, kind=kind, fn=fn,
                  deps=sorted(dep), tag=tag)
        self.ops.append(op)
        self.stream_ops[stream].append(i)
        for b in reads:
            b.readers.append(i)
        for b in writes:
            b.writers = [i]
            b.readers = []
        return i

    NSLOT = 8

    def emit(self, nc, sems):
        ops = self.ops
        needed = set()
        for op in ops:
            for d in op["deps"]:
                if ops[d]["stream"] != op["stream"]:
                    needed.add(d)
        counters = {}
        dma_idx = {}
        inc_val = {}
        self_wait = {}
        for op in ops:
            sem = _SEM_OF[(op["stream"], op["kind"])]
            if op["kind"] == "d":
                i = dma_idx.get(sem, 0)
                dma_idx[sem] = i + 1
                key = (sem, i % self.NSLOT)
                occ = i // self.NSLOT
                inc_val[op["id"]] = (key, 16 * (occ + 1))
                if occ > 0:
                    self_wait[op["id"]] = (key, 16 * occ)
            elif op["id"] in needed:
                counters[sem] = counters.get(sem, 0) + 1
                inc_val[op["id"]] = ((sem, 0), counters[sem])
        self._inc_val = inc_val

        def make_body(stream):
            def body(eng):
                waited = {}
                for i in self.stream_ops[stream]:
                    op = ops[i]
                    need = {}
                    for d in op["deps"]:
                        if ops[d]["stream"] == stream:
                            if ops[d]["kind"] == "c" or op["kind"] == "d":
                                continue
                        key, dval = inc_val[ops[d]["id"]]
                        if need.get(key, 0) < dval:
                            need[key] = dval
                    if i in self_wait:
                        key, dval = self_wait[i]
                        if need.get(key, 0) < dval:
                            need[key] = dval
                    if i == self.drain_op:
                        for oid, (key, dval) in inc_val.items():
                            if ops[oid]["kind"] == "d" and need.get(key, 0) < dval:
                                need[key] = dval
                    for key, dval in sorted(need.items()):
                        if waited.get(key, 0) < dval:
                            eng.wait_ge(sems[key], dval)
                            waited[key] = dval
                    inst = op["fn"](eng)
                    if i in inc_val:
                        assert inst is not None, f"op {i} ({op['tag']}) marked but returned no instruction"
                        key, _ = inc_val[i]
                        step = 16 if op["kind"] == "d" else 1
                        inst.then_inc(sems[key], step)
            return body
        return {s: make_body(s) for s in _STREAMS}



def build_program(phases=("A", "B", "C"), debug_taps=False,
                  a_seqs=BS, a_tiles=TPS, a_asm=True, a_stage=9):
    nc = bass.Bass("TRN2", target_bir_lowering=False,
                   detect_race_conditions=False)

    x_d = nc.dram_tensor("x", [BS, D, S], F32, kind="ExternalInput").ap()
    wq_d = nc.dram_tensor("wq", [D, D], F32, kind="ExternalInput").ap()
    wkv_d = nc.dram_tensor("wkv", [D, 2 * D], F32, kind="ExternalInput").ap()
    wo_d = nc.dram_tensor("wo", [D, D], F32, kind="ExternalInput").ap()
    wfcn_d = nc.dram_tensor("wfcn", [HID, D], F32, kind="ExternalInput").ap()
    wfct_d = nc.dram_tensor("wfct", [D, HID], F32, kind="ExternalInput").ap()
    wpjn_d = nc.dram_tensor("wpjn", [D, HID], F32, kind="ExternalInput").ap()
    wpjt_d = nc.dram_tensor("wpjt", [HID, D], F32, kind="ExternalInput").ap()
    vt_d = nc.dram_tensor("vt", [D, RK], F32, kind="ExternalInput").ap()
    fbt_d = nc.dram_tensor("fbt", [RK, RK], F32, kind="ExternalInput").ap()
    ut_d = nc.dram_tensor("ut", [RK, HID], F32, kind="ExternalInput").ap()
    gate_d = nc.dram_tensor("gate", [1, 1], F32, kind="ExternalInput").ap()
    selm_d = nc.dram_tensor("selm", [H, 1024], F32, kind="ExternalInput").ap()
    id128_d = nc.dram_tensor("id128", [128, 128], F32, kind="ExternalInput").ap()
    out_d = nc.dram_tensor("out", [BS, D, S], F32, kind="ExternalOutput").ap()
    xnew_kind = "ExternalOutput" if debug_taps else "Internal"
    xnew_d = nc.dram_tensor("xnew", [BS, D, S], F32, kind=xnew_kind).ap()
    scal_d = nc.dram_tensor("scal", [1, 5120], F32, kind="Internal").ap()
    if debug_taps:
        dq_d = nc.dram_tensor("dbg_q", [128, KD * BS * S], BF16, kind="ExternalOutput").ap()
        dkv_d = nc.dram_tensor("dbg_kvs", [128, 2048], BF16, kind="ExternalOutput").ap()
        dks_d = nc.dram_tensor("dbg_ksp", [128, 256], BF16, kind="ExternalOutput").ap()
        dwf_d = nc.dram_tensor("dbg_wfc", [128, KD * HID], BF16, kind="ExternalOutput").ap()
        dwp_d = nc.dram_tensor("dbg_wpj", [128, 32 * 1024], BF16, kind="ExternalOutput").ap()
        dsc_d = nc.dram_tensor("dbg_scal", [1, 5120], F32, kind="ExternalOutput").ap()

    r1 = nc.alloc_sbuf_tensor("r1", [128, 131072], U8)

    def carve(region, byte_off, nbytes, dtype, cols):
        apx = region.ap()[:, byte_off:byte_off + nbytes].bitcast(dtype)
        assert apx.shape[1] == cols, (apx.shape, cols)
        return apx

    wq = carve(r1, 0, 16384, BF16, KD * 1024)
    wkv = carve(r1, 16384, 32768, BF16, KD * 2048)
    wo = carve(r1, 49152, 16384, BF16, KD * 1024)
    qs = carve(r1, 65536, 65536, BF16, KD * 4096)
    wfc = carve(r1, 0, 65536, BF16, KD * 4096)
    wpj = carve(r1, 65536, 65536, BF16, 32 * 1024)

    rs = nc.alloc_sbuf_tensor("rs", [128, 47104], U8)
    xf = [carve(rs, 0, 8192, F32, 2048), carve(rs, 8192, 8192, F32, 2048)]
    xbf = [carve(rs, 16384, 4096, BF16, 2048), carve(rs, 20480, 4096, BF16, 2048)]
    xn = carve(rs, 24576, 4096, BF16, 2048)
    attn = carve(rs, 28672, 4096, BF16, 2048)
    zf = carve(rs, 32768, 8192, F32, 2048)
    zf_lo = carve(rs, 32768, 4096, F32, 1024)
    zf_hi = carve(rs, 36864, 4096, F32, 1024)
    zbf = carve(rs, 40960, 4096, BF16, 2048)
    bca = carve(rs, 45056, 1024, F32, 256)
    bcb = carve(rs, 46080, 1024, F32, 256)
    rs_bf = rs.ap().bitcast(BF16)
    RSROW = rs_bf.ap[0][0]

    def pair_view(x_byte_off, sq_byte_off):
        def f(k, x_byte_off=x_byte_off, sq_byte_off=sq_byte_off):
            lo = min(x_byte_off, sq_byte_off) // 2
            hi = max(x_byte_off, sq_byte_off) // 2
            return AP(rs_bf.tensor, lo + T * k,
                      [[RSROW, 128], [hi - lo, 2], [1, T]])
        return f

    rposf = carve(rs, 40960, 4096, F32, 1024)
    raw0 = carve(rs, 16384, 4096, F32, 1024)
    raw1 = carve(rs, 20480, 4096, F32, 1024)
    awf = carve(rs, 40960, 4096, F32, 1024)
    sgf = carve(rs, 36864, 4096, F32, 1024)
    m1f = carve(rs, 24576, 4096, F32, 1024)
    msf = carve(rs, 28672, 4096, F32, 1024)
    awf1 = carve(rs, 32768, 4096, F32, 1024)
    sgf1 = carve(rs, 0, 4096, F32, 1024)
    m1f1 = carve(rs, 4096, 4096, F32, 1024)
    msf1 = carve(rs, 8192, 4096, F32, 1024)

    r2 = nc.alloc_sbuf_tensor("r2", [128, 16384], U8)
    kbf = carve(r2, 0, 2048, BF16, 1024)
    vbf = carve(r2, 2048, 2048, BF16, 1024)
    wnat0a = carve(r2, 4096, 4096, F32, 1024)
    wnat0b = carve(r2, 8192, 4096, F32, 1024)
    wnat1q = carve(r2, 12288, 4096, F32, 1024)
    r2s = carve(r2, 0, 16384, BF16, 8192)

    rk = nc.alloc_sbuf_tensor("rk", [128, 8320], U8)
    kv_sb = rk.ap()[0:64, 0:4096].bitcast(F32)
    ksum_sb = rk.ap()[0:64, 4096:4160].bitcast(F32)
    prow = [rk.ap()[0:1, 0:2048].bitcast(F32),
            rk.ap()[0:1, 2048:4096].bitcast(F32)]
    kvs2 = rk.ap()[:, 4224:8320].bitcast(BF16)
    ut = rk.ap()[0:16, 0:8192].bitcast(BF16)

    kspad_t = nc.alloc_sbuf_tensor("kspad", [128, 256], BF16)
    selm_t = nc.alloc_sbuf_tensor("selm_s", [16, 1024], BF16)
    rows_t = nc.alloc_sbuf_tensor("rows", [1, 768], F32)
    sreg = nc.alloc_sbuf_tensor("sreg", [16, 2048], U8)
    z1_ap = sreg.ap()[0:16, 0:512].bitcast(BF16)
    z2_ap = sreg.ap()[0:16, 512:1024].bitcast(BF16)
    rnf_ap = sreg.ap()[0:16, 1024:2048].bitcast(F32)
    rnb2_ap = sreg.ap()[0:16, 512:1024].bitcast(BF16)
    rnb_ap = sreg.ap()[0:16, 0:512].bitcast(BF16)
    vt_t = nc.alloc_sbuf_tensor("vt_s", [128, 128], BF16)
    fbt_t = nc.alloc_sbuf_tensor("fbt_s", [16, 16], BF16)
    gatev_t = nc.alloc_sbuf_tensor("gatev", [16, 1], F32)
    vones_t = nc.alloc_sbuf_tensor("vones", [16, 1], F32)
    vxs = sreg.ap()[0:16, 1024:2048].bitcast(F32)
    gates_t = nc.alloc_sbuf_tensor("gates", [1, 1], F32)
    onesb_t = nc.alloc_sbuf_tensor("onesb", [128, 1], BF16)
    onesf_t = nc.alloc_sbuf_tensor("onesf", [1, 128], F32)
    epsc_t = nc.alloc_sbuf_tensor("epsc", [128, 1], F32)
    sccf_t = nc.alloc_sbuf_tensor("sccf", [128, 32], F32)
    sccp_t = nc.alloc_sbuf_tensor("sccp", [128, 8], F32)
    scpt_t = nc.alloc_sbuf_tensor("scpt", [128, 32], F32)
    sct_ap = r2.ap()[0:40, 0:512].bitcast(F32)
    id_ap = r2.ap()[:, 512:1024].bitcast(F32)
    scb = r2.ap()[:, 1024:5120].bitcast(F32)
    scb2 = rs.ap()[:, 28672:32768].bitcast(F32)
    tchain0 = r2.ap()[:, 8192:12288].bitcast(F32)
    tsg0 = r2.ap()[:, 12288:16384].bitcast(F32)
    zf2 = r2.ap()[:, 8192:16384].bitcast(F32)
    rawpair = rs.ap()[:, 16384:24576].bitcast(F32)
    tsg0a = r2.ap()[:, 12288:14336].bitcast(mybir.dt.bfloat16)
    tsg0b = r2.ap()[:, 14336:16384].bitcast(mybir.dt.bfloat16)
    zfha = rs.ap()[:, 36864:38912].bitcast(mybir.dt.bfloat16)
    zfhb = rs.ap()[:, 38912:40960].bitcast(mybir.dt.bfloat16)
    rawpair2 = rs.ap()[:, 32768:40960].bitcast(F32)
    tch2 = carve(rs, 8192, 4096, F32, 1024)
    tsg2 = carve(rs, 12288, 4096, F32, 1024)
    tch3 = carve(rs, 28672, 4096, F32, 1024)
    tsg3 = carve(rs, 24576, 4096, F32, 1024)

    p4 = nc.alloc_psum_tensor("p4", [128, 2048], F32)
    p2 = nc.alloc_psum_tensor("p2", [128, 1024], F32)
    p1a = nc.alloc_psum_tensor("p1a", [128, 512], F32)
    p1b = nc.alloc_psum_tensor("p1b", [128, 512], F32)

    P = Prog()
    B_ = {}
    for nm in ("wq", "wkv", "wo", "qs", "xf0", "xf1", "xbf0", "xbf1", "xn",
               "attn", "zf_lo", "zf_hi", "zbf", "bca", "bcb",
               "r2a", "r2b", "r2c", "r2d", "kvsb", "kvs2", "kspad", "selm",
               "rows", "rnf", "rnb", "vt", "fbt", "z1", "z2", "gatev",
               "vones",
               "gates", "ones", "eps", "sccf", "sccp", "scpt", "sct", "id",
               "p4a", "p4b", "p4c", "p4d", "p2a", "p2b", "p1a", "p1b",
               "xdram", "odram", "scdram1", "scdram2", "prow0", "prow1",
               "tsga", "tsgb", "zfha", "zfhb",
               *[f"pj{k}" for k in range(32)],
               *[f"fck{k}q{qt}" for k in range(8) for qt in range(4)],
               *[f"xnd{i}" for i in range(16)],
               *[f"qst{i}" for i in range(16)]):
        B_[nm] = Buf(nm, exclusive=nm.startswith(("p4", "p2", "p1")))
    bufs = lambda *names: [B_[n] for n in names]
    xfB = [B_["xf0"], B_["xf1"]]
    xbfB = [B_["xbf0"], B_["xbf1"]]

    def p4bufs(c0, c1):
        out = []
        for j in range(4):
            if c0 < (j + 1) * 512 and c1 > j * 512:
                out.append(B_["p4" + "abcd"[j]])
        return out

    def p2bufs(c0, c1):
        out = []
        if c0 < 512:
            out.append(B_["p2a"])
        if c1 > 512:
            out.append(B_["p2b"])
        return out

    def x_tile(b, t0):
        return x_d.rearrange("b (k p) s -> p b k s", p=128)[:, b, :, t0:t0 + T]

    def xnew_tile(b, t0):
        return xnew_d.rearrange("b (k p) s -> p b k s", p=128)[:, b, :, t0:t0 + T]

    def out_tile(b, t0):
        return out_d.rearrange("b (k p) s -> p b k s", p=128)[:, b, :, t0:t0 + T]

    sems = {}

    def ph0():
        P.add("dve", "c", lambda e: e.memset(onesb_t.ap(), 1.0),
              writes=bufs("ones"), tag="memset_onesb")
        P.add("dve", "c", lambda e: e.memset(onesf_t.ap(), 1.0),
              writes=bufs("ones"), tag="memset_onesf")
        P.add("dve", "c", lambda e: e.memset(epsc_t.ap(), 1e-5),
              writes=bufs("eps"), tag="memset_eps")
        P.add("sync", "d", lambda e: e.dma_start(out=gates_t.ap(), in_=gate_d),
              writes=bufs("gates"), tag="ld_gate")
        P.add("gpsimd", "d", lambda e: e.dma_start(
            out=wq.rearrange("p (k o) -> p k o", k=KD),
            in_=wq_d.rearrange("(k p) o -> p k o", p=128)),
            writes=bufs("wq"), tag="ld_wq")
        P.add("gpsimd", "d", lambda e: e.dma_start(
            out=wkv.rearrange("p (k o) -> p k o", k=KD),
            in_=wkv_d.rearrange("(k p) o -> p k o", p=128)),
            writes=bufs("wkv"), tag="ld_wkv")
        P.add("gpsimd", "d", lambda e: e.dma_start(
            out=wo.rearrange("p (k o) -> p k o", k=KD),
            in_=wo_d.rearrange("(k p) o -> p k o", p=128)),
            writes=bufs("wo"), tag="ld_wo")
        P.add("gpsimd", "d", lambda e: e.dma_start(
            out=vt_t.ap().rearrange("p (k r) -> p k r", k=KD),
            in_=vt_d.rearrange("(k p) r -> p k r", p=128)),
            writes=bufs("vt"), tag="ld_vt")
        P.add("gpsimd", "d", lambda e: e.dma_start(out=fbt_t.ap(), in_=fbt_d),
              writes=bufs("fbt"), tag="ld_fbt")
        P.add("gpsimd", "d", lambda e: e.dma_start(out=selm_t.ap(), in_=selm_d),
              writes=bufs("selm"), tag="ld_selm")

    def lnorm_bcast(tag):
        def mm_bcast(e):
            return e.matmul(p1a.ap()[:, 0:2 * T], onesf_t.ap(),
                            rows_t.ap()[0:1, 0:2 * T], start=True, stop=True)
        P.add("pe", "c", mm_bcast, reads=bufs("rows", "ones"),
              writes=bufs("p1a"), tag=tag + "_bcast")

    def lnorm_stats(pair_rhs_fn, dep_bufs, tag, x_first=True,
                    split_bcast=False):
        xo, so = (0, T) if x_first else (T, 0)

        def mm_stats(e):
            for k in range(KD):
                i = e.matmul(p1a.ap()[0:1, 0:2 * T], onesb_t.ap(),
                             pair_rhs_fn(k),
                             start=(k == 0), stop=(k == KD - 1))
            return i
        P.add("pe", "c", mm_stats, reads=dep_bufs + bufs("ones"),
              writes=bufs("p1a"), tag=tag + "_mm")

        def rows_dve(e):
            r = rows_t.ap()
            e.tensor_scalar(r[0:1, 0:T], p1a.ap()[0:1, xo:xo + T],
                            1.0 / D, None, ALU.mult)
            e.tensor_scalar(r[0:1, T:2 * T], p1a.ap()[0:1, so:so + T],
                            1.0 / D, None, ALU.mult)
            e.tensor_tensor(r[0:1, 2 * T:3 * T], r[0:1, 0:T], r[0:1, 0:T],
                            ALU.mult)
            return e.tensor_tensor(r[0:1, T:2 * T], r[0:1, T:2 * T],
                                   r[0:1, 2 * T:3 * T], ALU.subtract)
        P.add("dve", "c", rows_dve, reads=bufs("p1a"),
              writes=bufs("rows"), tag=tag + "_rows")

        def rows_act(e):
            r = rows_t.ap()
            e.activation(r[0:1, 2 * T:3 * T], r[0:1, T:2 * T], AF.Ln,
                         bias=epsc_t.ap()[0:1, 0:1])
            return e.activation(r[0:1, T:2 * T], r[0:1, 2 * T:3 * T], AF.Exp,
                                scale=-0.5)
        P.add("scalar", "c", rows_act, reads=bufs("rows", "eps"),
              writes=bufs("rows"), tag=tag + "_rstd")

        if not split_bcast:
            lnorm_bcast(tag)

    def ln_apply(xf_ap, out_ap, xf_bufs, out_bufs, tag):
        x3 = xf_ap.rearrange("p (k t) -> p k t", k=KD)
        z3 = zf.rearrange("p (k t) -> p k t", k=KD)
        o3 = out_ap.rearrange("p (k t) -> p k t", k=KD)
        mb = p1a.ap()[:, 0:T].unsqueeze(1).broadcast_to([128, KD, T])
        rb = p1a.ap()[:, T:2 * T].unsqueeze(1).broadcast_to([128, KD, T])

        def f(e):
            e.scalar_tensor_tensor(z3, x3, 0.0, mb, ALU.bypass, ALU.subtract)
            return e.scalar_tensor_tensor(o3, z3, 0.0, rb, ALU.bypass,
                                          ALU.mult)
        P.add("dve", "c", f,
              reads=xf_bufs + bufs("p1a"),
              writes=out_bufs + bufs("zf_lo", "zf_hi"), tag=tag)

    def elu1(psum_ap, psum_bufs, out_ap, out_bufs, width, tag):
        h = width // 2
        for s in range(2):
            pz = psum_ap[:, s * h:s * h + h]
            if len(out_ap.ap) == 3:
                oz = out_ap[:, 2 * s:2 * s + 2, :]
            else:
                oz = out_ap[:, s * h:s * h + h]
            pb = [psum_bufs[s]] if len(psum_bufs) == 2 else psum_bufs

            def a1(e, pz=pz, s=s):
                i = e.activation(zf_lo[:, s * h:s * h + h], pz, AF.Relu,
                                 scale=-1.0)
                if "e" in ELU_PARTS:
                    i = e.activation(zf_hi[:, s * h:s * h + h],
                                     zf_lo[:, s * h:s * h + h],
                                     AF.Exp, scale=-1.0)
                return i
            P.add("scalar", "c", a1, reads=pb,
                  writes=bufs("zf_lo", "zf_hi"), tag=tag + f"_exp{s}")

            def d1(e, pz=pz, oz=oz, s=s):
                return e.scalar_tensor_tensor(oz, pz, 0.0,
                                              zf_hi[:, s * h:s * h + h],
                                              ALU.max, ALU.add)
            P.add("dve", "c", d1, reads=pb + bufs("zf_hi"),
                  writes=out_bufs, tag=tag + f"_relu_add{s}")

    def phaseA():
        tiles = [(b, ti) for b in range(a_seqs) for ti in range(a_tiles)]

        def a_load(idx):
            b, ti = tiles[idx]
            t0 = ti * T
            par = idx % 2
            tg = f"A{b}_{ti}"
            P.add("sync", "d", lambda e, b=b, t0=t0, par=par: e.dma_start(
                out=xf[par].rearrange("p (k t) -> p k t", k=KD),
                in_=x_tile(b, t0)), writes=[xfB[par]], tag=tg + "_ldx")
            P.add("gpsimd", "d", lambda e, b=b, t0=t0, par=par: e.dma_start(
                out=xbf[par].rearrange("p (k t) -> p k t", k=KD),
                in_=x_tile(b, t0)), writes=[xbfB[par]], tag=tg + "_ldxb")

        def a_xsq(idx):
            b, ti = tiles[idx]
            par = idx % 2
            P.add("scalar", "c", lambda e, par=par: e.activation(
                attn, xf[par], AF.Square),
                reads=[xfB[par]], writes=bufs("attn"), tag=f"A{b}_{ti}_xsq")

        def a_stats(idx):
            b, ti = tiles[idx]
            par = idx % 2
            tg = f"A{b}_{ti}"
            lnorm_stats(pair_view(16384 + 4096 * par, 28672),
                        [xbfB[par], B_["attn"]], tg + "_ln1", split_bcast=True)

        def a_bcast(idx):
            b, ti = tiles[idx]
            lnorm_bcast(f"A{b}_{ti}_ln1")

        def a_xn(idx):
            b, ti = tiles[idx]
            par = idx % 2
            ln_apply(xf[par], xn, [xfB[par]], bufs("xn"), f"A{b}_{ti}_xn")

        def a_q(idx):
            b, ti = tiles[idx]
            t0 = ti * T
            tg = f"A{b}_{ti}"
            xn3 = xn.rearrange("p (k t) -> p k t", k=KD)
            for g in range(2):
                cols0 = 1024 * g

                def mm_q(e, g=g, cols0=cols0, xn3=xn3):
                    for mi in range(4):
                        m = 4 * g + mi
                        for k in range(KD):
                            i = e.matmul(
                                p4.ap()[:, cols0 + 256 * mi: cols0 + 256 * mi + 256],
                                wq[:, 1024 * k + 128 * m: 1024 * k + 128 * m + 128],
                                xn3[:, k, :],
                                start=(mi % 2 == 0 and k == 0),
                                stop=(mi % 2 == 1 and k == KD - 1))
                    return i
                pb = p4bufs(cols0, cols0 + 1024)
                P.add("pe", "c", mm_q, reads=bufs("xn", "wq"), writes=pb,
                      tag=tg + f"_qmm{g}")
                qdst = AP(qs.tensor, qs.offset + 4096 * 4 * g + 2048 * b + t0,
                          [[qs.ap[0][0], 128], [4096, 4], [1, T]])
                elu1(p4.ap()[:, cols0:cols0 + 1024], pb,
                     qdst, bufs("qs", f"qst{b * TPS + ti}"), 1024,
                     tg + f"_qel{g}")

        def a_kv_mm(idx, g2):
            b, ti = tiles[idx]
            tg = f"A{b}_{ti}"
            xn3 = xn.rearrange("p (k t) -> p k t", k=KD)

            def mm_k(e, g2=g2, xn3=xn3):
                for n2 in range(2):
                    for k in range(KD):
                        i = e.matmul(
                            p2.ap()[:, 512 * n2:512 * n2 + 512],
                            xn3[:, k, 128 * g2:128 * g2 + 128],
                            wkv[:, 2048 * k + 512 * n2: 2048 * k + 512 * n2 + 512],
                            start=(k == 0), stop=(k == KD - 1))
                return i
            P.add("pe", "c", mm_k, reads=bufs("xn", "wkv"),
                  writes=bufs("p2a", "p2b"), tag=tg + f"_kmm{g2}")

        def a_kv_kel(idx, g2):
            b, ti = tiles[idx]
            elu1(p2.ap()[:, 0:1024], bufs("p2a", "p2b"),
                 kbf, bufs("r2a"), 1024, f"A{b}_{ti}_kel{g2}")

        def a_kv_vm(idx, g2, hook=None):
            b, ti = tiles[idx]
            tg = f"A{b}_{ti}"
            xn3 = xn.rearrange("p (k t) -> p k t", k=KD)

            def mm_v(e, g2=g2, xn3=xn3):
                for n2 in range(2):
                    for k in range(KD):
                        i = e.matmul(
                            p4.ap()[:, 512 * n2:512 * n2 + 512],
                            xn3[:, k, 128 * g2:128 * g2 + 128],
                            wkv[:, 2048 * k + 1024 + 512 * n2: 2048 * k + 1024 + 512 * n2 + 512],
                            start=(k == 0), stop=(k == KD - 1))
                return i
            P.add("pe", "c", mm_v, reads=bufs("xn", "wkv"),
                  writes=p4bufs(0, 1024), tag=tg + f"_vmm{g2}")
            P.add("scalar", "c", lambda e: e.activation(
                vbf[:, 0:512], p4.ap()[:, 0:512], AF.Copy),
                reads=p4bufs(0, 512), writes=bufs("r2b"),
                tag=tg + f"_vcpA{g2}")
            P.add("scalar", "c", lambda e: e.activation(
                vbf[:, 512:1024], p4.ap()[:, 512:1024], AF.Copy),
                reads=p4bufs(512, 1024), writes=bufs("r2b"),
                tag=tg + f"_vcpB{g2}")
            if hook is not None:
                hook()

        def a_kv_upd(idx, g2):
            b, ti = tiles[idx]
            tg = f"A{b}_{ti}"

            def mm_kv(e):
                for h in range(H):
                    i = e.matmul(
                        p4.ap()[0:64, 1024 + 64 * h: 1024 + 64 * h + 64],
                        kbf[:, 64 * h:64 * h + 64],
                        vbf[:, 64 * h:64 * h + 64],
                        start=(h % 8 == 0), stop=(h % 8 == 7))
                return i
            P.add("pe", "c", mm_kv, reads=bufs("r2a", "r2b"),
                  writes=p4bufs(1024, 2048), tag=tg + f"_kvmm{g2}")

            def mm_ks(e):
                for h in range(H):
                    i = e.matmul(
                        p1b.ap()[0:64, h:h + 1],
                        kbf[:, 64 * h:64 * h + 64],
                        onesb_t.ap(),
                        start=(h == 0), stop=(h == H - 1))
                return i
            P.add("pe", "c", mm_ks, reads=bufs("r2a", "ones"),
                  writes=bufs("p1b"), tag=tg + f"_ksmm{g2}")

            def acc(e):
                e.tensor_tensor(kv_sb, kv_sb, p4.ap()[0:64, 1024:2048],
                                ALU.add)
                return e.tensor_tensor(ksum_sb, ksum_sb,
                                       p1b.ap()[0:64, 0:16], ALU.add)
            P.add("dve", "c", acc,
                  reads=p4bufs(1024, 2048) + bufs("p1b", "kvsb"),
                  writes=bufs("kvsb"), tag=tg + f"_acc{g2}")

        def a_seqend(b):
            if not a_asm:
                return
            P.add("dve", "c", lambda e, b=b: e.memset(
                kvs2[:, 1024 * b:1024 * b + 1024], 0.0),
                writes=bufs("kvs2"), tag=f"A{b}_kvz")
            P.add("dve", "c", lambda e, b=b: e.memset(
                kspad_t.ap()[:, 128 * b:128 * b + 128], 0.0),
                writes=bufs("kspad"), tag=f"A{b}_ksz")

            def kv_even(e, b=b):
                src = AP(kv_sb.tensor, kv_sb.offset,
                         [[kv_sb.ap[0][0], 64], [128, KD], [1, 64]])
                dst = AP(kvs2.tensor, kvs2.offset + 1024 * b,
                         [[kvs2.ap[0][0], 64], [128, KD], [1, 64]])
                return e.dma_start(out=dst, in_=src)
            P.add("gpsimd", "d", kv_even, reads=bufs("kvsb"),
                  writes=bufs("kvs2"), tag=f"A{b}_kve")

            def kv_odd(e, b=b):
                src = AP(kv_sb.tensor, kv_sb.offset + 64,
                         [[kv_sb.ap[0][0], 64], [128, KD], [1, 64]])
                t = kvs2[64:128, 1024 * b:1024 * b + 1024]
                dst = AP(t.tensor, t.offset + 64,
                         [[t.ap[0][0], 64], [128, KD], [1, 64]])
                return e.dma_start(out=dst, in_=src)
            P.add("gpsimd", "d", kv_odd, reads=bufs("kvsb"),
                  writes=bufs("kvs2"), tag=f"A{b}_kvo")

            def ks_even(e, b=b):
                src = AP(ksum_sb.tensor, ksum_sb.offset,
                         [[ksum_sb.ap[0][0], 64], [2, KD], [1, 1]])
                t = kspad_t.ap()[0:64, 128 * b:128 * b + 128]
                dst = AP(t.tensor, t.offset, [[t.ap[0][0], 64], [18, KD], [1, 1]])
                return e.dma_start(out=dst, in_=src,
                                   allow_slow_non_contiguous=True)
            P.add("gpsimd", "d", ks_even, reads=bufs("kvsb"),
                  writes=bufs("kspad"), tag=f"A{b}_kse")

            def ks_odd(e, b=b):
                src = AP(ksum_sb.tensor, ksum_sb.offset + 1,
                         [[ksum_sb.ap[0][0], 64], [2, KD], [1, 1]])
                t = kspad_t.ap()[64:128, 128 * b:128 * b + 128]
                dst = AP(t.tensor, t.offset + 1,
                         [[t.ap[0][0], 64], [18, KD], [1, 1]])
                return e.dma_start(out=dst, in_=src,
                                   allow_slow_non_contiguous=True)
            P.add("gpsimd", "d", ks_odd, reads=bufs("kvsb"),
                  writes=bufs("kspad"), tag=f"A{b}_kso")

        def a_scale(it):
            tg = f"As{it}"
            for s in range(2):
                c0 = 2 * it + s
                wn = wnat0a if s == 0 else wnat0b
                wnB = bufs("r2b") if s == 0 else bufs("r2c")
                P.add("sync", "d", lambda e, c0=c0, wn=wn: e.dma_start(
                    out=wn,
                    in_=wfcn_d.rearrange("(c p) i -> p c i", p=128)[:, c0, :]),
                    writes=wnB, tag=tg + f"_ldn{s}")
                P.add("dve", "c", lambda e, c0=c0, wn=wn: e.tensor_reduce(
                    sccf_t.ap()[:, c0:c0 + 1], wn, AX.X, ALU.add,
                    apply_absolute_value=True),
                    reads=wnB, writes=bufs("sccf"), tag=tg + f"_red{s}")
            for s in range(2):
                qq = 2 * it + s
                j, u = qq // 4, qq % 4
                P.add("sync", "d", lambda e, j=j, u=u: e.dma_start(
                    out=wnat1q,
                    in_=wpjn_d.rearrange("(c p) (u q) -> p c u q",
                                         p=128, q=1024)[:, j, u, :]),
                    writes=bufs("r2d"), tag=tg + f"_ldp{s}")
                P.add("dve", "c", lambda e, qq=qq: e.tensor_reduce(
                    scpt_t.ap()[:, qq:qq + 1], wnat1q, AX.X, ALU.add,
                    apply_absolute_value=True),
                    reads=bufs("r2d"), writes=bufs("scpt"), tag=tg + f"_rdp{s}")

        n = len(tiles)
        if n:
            a_load(0)
            if n > 1:
                a_load(1)
            a_xsq(0)
            a_stats(0)
            a_bcast(0)
            a_xn(0)
        for idx in range(n):
            b, ti = tiles[idx]
            if ti == 0:
                P.add("dve", "c", lambda e: e.memset(kv_sb, 0.0),
                      writes=bufs("kvsb"), tag=f"kv0_{b}")
                P.add("dve", "c", lambda e: e.memset(ksum_sb, 0.0),
                      writes=bufs("kvsb"), tag=f"ks0_{b}")
            if idx + 2 < n:
                a_load(idx + 2)
            if idx + 1 < n:
                a_xsq(idx + 1)
            a_q(idx)
            if idx + 1 < n:
                a_stats(idx + 1)
            a_kv_mm(idx, 0)
            a_kv_kel(idx, 0)
            a_kv_vm(idx, 0)
            if idx + 1 < n:
                a_bcast(idx + 1)
            a_kv_mm(idx, 1)
            a_kv_upd(idx, 0)
            a_kv_kel(idx, 1)
            a_kv_vm(idx, 1,
                    hook=(lambda idx=idx: a_xn(idx + 1))
                    if idx + 1 < n else None)
            a_kv_upd(idx, 1)
            a_scale(idx)
            if ti == a_tiles - 1:
                a_seqend(b)
        if a_asm and n == len([(b, t) for b in range(BS) for t in range(TPS)]):
            fin_scales()

    def phaseB():
        tern_items = []
        for qt in range(4):
            tern_items.append((lambda qt=qt: tern_prep(
                1024 * qt, f"tfq{qt}", use_prow=True), True))
            for k in (0, 2, 4):
                tern_items.append((lambda qt=qt, k=k:
                                   fc_tern_qt(qt, [k], post_b=False), False))
        tern_sched = {i: fp for i, fp in enumerate(tern_items)}

        def bn_norm(it):
            b, t0 = it // TPS, (it % TPS) * T
            pn, pnB = (p1a, bufs("p1a")) if it % 2 == 0 else (p1b, bufs("p1b"))
            qsl = lambda c, b=b, t0=t0: qs[:, 4096 * c + 2048 * b + t0:
                                           4096 * c + 2048 * b + t0 + T]

            def mm_nrm(e, b=b, qsl=qsl, pn=pn):
                for c in range(KD):
                    i = e.matmul(pn.ap()[0:16, 0:T],
                                 kspad_t.ap()[:, 128 * b + 16 * c:128 * b + 16 * c + 16],
                                 qsl(c), start=(c == 0), stop=(c == KD - 1))
                return i
            P.add("pe", "c", mm_nrm, reads=bufs(f"qst{it}", "kspad"),
                  writes=pnB, tag=f"Bn{it}_nmm")

        def bn_rn(it):
            pn, pnB = (p1a, bufs("p1a")) if it % 2 == 0 else (p1b, bufs("p1b"))
            rb = rnb_ap if it % 2 == 0 else rnb2_ap
            rbB = bufs("rnb" if it % 2 == 0 else "z2")

            def rn_ops(e, pn=pn, rb=rb):
                e.tensor_scalar(rnf_ap, pn.ap()[0:16, 0:T], 1e-6,
                                None, ALU.max)
                with nc.allow_low_precision(reason="normalizer recip; post-LN scale-invariant"):
                    return e.reciprocal(rb, rnf_ap)
            P.add("dve", "c", rn_ops, reads=pnB,
                  writes=bufs("rnf") + rbB, tag=f"Bn{it}_rn")

        def bn_div(it):
            b, t0 = it // TPS, (it % TPS) * T
            tg = f"Bn{it}"
            rb = rnb_ap if it % 2 == 0 else rnb2_ap
            rbB = bufs("rnb" if it % 2 == 0 else "z2")

            def mm_div(e, rb=rb):
                for c in range(KD):
                    if c < 4:
                        dst = p2.ap()[:, 256 * c:256 * c + 256]
                    else:
                        dst = p4.ap()[:, 256 * (c - 4):256 * (c - 4) + 256]
                    i = e.matmul(dst, selm_t.ap()[:, 128 * c:128 * c + 128],
                                 rb, start=(c % 2 == 0), stop=(c % 2 == 1))
                return i
            P.add("pe", "c", mm_div, reads=rbB + bufs("selm"),
                  writes=bufs("p2a", "p2b", "p4a", "p4b"), tag=tg + "_dmm")

            def div_evac(e):
                e.activation(xn[:, 0:1024], p2.ap()[:, 0:1024], AF.Copy)
                return e.activation(xn[:, 1024:2048], p4.ap()[:, 0:1024],
                                    AF.Copy)
            P.add("scalar", "c", div_evac,
                  reads=bufs("p2a", "p2b", "p4a", "p4b"),
                  writes=bufs("xn"), tag=tg + "_dev")

        def bn_qmul(it):
            b, t0 = it // TPS, (it % TPS) * T
            qv = AP(qs.tensor, qs.offset + 2048 * b + t0,
                    [[qs.ap[0][0], 128], [4096, KD], [1, T]])
            P.add("dve", "c", lambda e, qv=qv: e.scalar_tensor_tensor(
                qv, qv, 0.0, xn.rearrange("p (k t) -> p k t", k=KD),
                ALU.bypass, ALU.mult),
                reads=bufs("xn", f"qst{it}"),
                writes=bufs("qs", f"qst{it}"), tag=f"Bn{it}_qmul")

        P.add("dve", "c", lambda e: e.tensor_copy(wfc[0:1, 0:1],
                                                  wfc[0:1, 0:1]),
              writes=bufs("wq", "wkv") +
              [B_[f"fck{k}q{qt}"] for k in range(6)
               for qt in range(4)], tag="fence_wqkv")

        bn_norm(0)
        bn_norm(1)
        bn_rn(0)
        bn_rn(1)
        for it in range(0, NTILE, 2):
            if it + 2 < NTILE:
                bn_norm(it + 2)
                bn_norm(it + 3)
            bn_div(it)
            bn_qmul(it)
            tern_sched[it][0]()
            bn_div(it + 1)
            bn_qmul(it + 1)
            if it + 2 < NTILE:
                bn_rn(it + 2)
                bn_rn(it + 3)
            tern_sched[it + 1][0]()

        zfp = [zf, zf2]
        zfB = [bufs("zf_lo", "zf_hi"), bufs("r2c", "r2d")]
        zfBlo = [bufs("zf_lo"), bufs("r2c")]
        zfBhi = [bufs("zf_hi"), bufs("r2d")]

        def b_ldx(it):
            b, ti = it // TPS, it % TPS
            par = it % 2
            P.add("sync", "d", lambda e, b=b, t0=ti * T, par=par: e.dma_start(
                out=xf[par].rearrange("p (k t) -> p k t", k=KD),
                in_=x_tile(b, t0)), writes=[xfB[par]],
                tag=f"B{b}_{ti}_ldx")

        def b_zevA(it):
            p = it % 2
            tg = f"B{it // TPS}_{it % TPS}"
            P.add("dve", "c", lambda e, p=p: e.tensor_copy(
                zfp[p][:, 0:1024], p2.ap()[:, 0:1024]),
                reads=bufs("p2a", "p2b"), writes=zfBlo[p], tag=tg + "_zevA")

        def b_zevB(it):
            p = it % 2
            tg = f"B{it // TPS}_{it % TPS}"
            P.add("dve", "c", lambda e, p=p: (
                e.tensor_copy(zfp[p][:, 1024:1536], p1a.ap()[:, 0:512]),
                e.tensor_copy(zfp[p][:, 1536:2048], p1b.ap()[:, 0:512]))[-1],
                reads=bufs("p1a", "p1b"), writes=zfBhi[p], tag=tg + "_zevB")

        def b_casts(it):
            p = it % 2
            tg = f"B{it // TPS}_{it % TPS}"

            def casts(e, p=p):
                e.activation(zbf, zfp[p], AF.Copy)
                return e.activation(xn, zfp[p], AF.Square)
            P.add("scalar", "c", casts, reads=zfB[p],
                  writes=bufs("zbf", "xn"), tag=tg + "_casts")

        def b_omm(it):
            b, ti = it // TPS, it % TPS
            t0 = ti * T
            tg = f"B{b}_{ti}"
            qsl = lambda c, b=b, t0=t0: qs[:, 4096 * c + 2048 * b + t0:
                                           4096 * c + 2048 * b + t0 + T]
            qrd = bufs(f"qst{it}", "kvs2") if it < NTILE - 1 else \
                bufs(f"qst{it}", "kvs2", "qs")

            def mm_outA(e, b=b, qsl=qsl):
                for c in range(4):
                    i = e.matmul(p4.ap()[:, 256 * c:256 * c + 256],
                                 kvs2[:, 1024 * b + 128 * c:1024 * b + 128 * c + 128],
                                 qsl(c), start=(c % 2 == 0), stop=(c % 2 == 1))
                return i
            P.add("pe", "c", mm_outA, reads=qrd, writes=p4bufs(0, 1024),
                  tag=tg + "_ommA")

            def mm_outB(e, b=b, qsl=qsl):
                for c in range(4, KD):
                    i = e.matmul(p4.ap()[:, 256 * c:256 * c + 256],
                                 kvs2[:, 1024 * b + 128 * c:1024 * b + 128 * c + 128],
                                 qsl(c), start=(c % 2 == 0), stop=(c % 2 == 1))
                return i
            P.add("pe", "c", mm_outB, reads=qrd, writes=p4bufs(1024, 2048),
                  tag=tg + "_ommB")

        def b_oev(it):
            tg = f"B{it // TPS}_{it % TPS}"
            P.add("scalar", "c", lambda e: e.activation(
                attn[:, 0:1024], p4.ap()[:, 0:1024], AF.Copy),
                reads=p4bufs(0, 1024), writes=bufs("attn"), tag=tg + "_oevA")
            P.add("scalar", "c", lambda e: e.activation(
                attn[:, 1024:2048], p4.ap()[:, 1024:2048], AF.Copy),
                reads=p4bufs(1024, 2048), writes=bufs("attn"),
                tag=tg + "_oevB")

        at3 = attn.rearrange("p (k t) -> p k t", k=KD)

        def b_wmmA(it):
            tg = f"B{it // TPS}_{it % TPS}"

            def mm_woA(e):
                for m in range(4):
                    dst = p2.ap()[:, 256 * m:256 * m + 256]
                    for k in range(KD):
                        i = e.matmul(dst,
                                     wo[:, 1024 * k + 128 * m:1024 * k + 128 * m + 128],
                                     at3[:, k, :],
                                     start=(m % 2 == 0 and k == 0),
                                     stop=(m % 2 == 1 and k == KD - 1))
                return i
            P.add("pe", "c", mm_woA, reads=bufs("attn", "wo"),
                  writes=bufs("p2a", "p2b"), tag=tg + "_wmmA")

        def b_wmmB(it):
            tg = f"B{it // TPS}_{it % TPS}"

            def mm_woB(e):
                for m in range(4, KD):
                    if m < 6:
                        dst = p1a.ap()[:, 256 * (m - 4):256 * (m - 4) + 256]
                    else:
                        dst = p1b.ap()[:, 256 * (m - 6):256 * (m - 6) + 256]
                    for k in range(KD):
                        i = e.matmul(dst,
                                     wo[:, 1024 * k + 128 * m:1024 * k + 128 * m + 128],
                                     at3[:, k, :],
                                     start=(m % 2 == 0 and k == 0),
                                     stop=(m % 2 == 1 and k == KD - 1))
                return i
            P.add("pe", "c", mm_woB, reads=bufs("attn", "wo"),
                  writes=bufs("p1a", "p1b"), tag=tg + "_wmmB")

        def b_stats_mm(it):
            tg = f"B{it // TPS}_{it % TPS}"
            pv = pair_view(40960, 24576)

            def mm_stats(e, pv=pv):
                for k in range(KD):
                    i = e.matmul(p4.ap()[0:1, 1536:2048], onesb_t.ap(),
                                 pv(k), start=(k == 0), stop=(k == KD - 1))
                return i
            P.add("pe", "c", mm_stats, reads=bufs("zbf", "xn", "ones"),
                  writes=bufs("p4d"), tag=tg + "_ln2_mm")

            def rows_dve(e):
                r = rows_t.ap()
                e.tensor_scalar(r[0:1, 0:T], p4.ap()[0:1, 1792:1792 + T],
                                1.0 / D, None, ALU.mult)
                e.tensor_scalar(r[0:1, T:2 * T], p4.ap()[0:1, 1536:1536 + T],
                                1.0 / D, None, ALU.mult)
                e.tensor_tensor(r[0:1, 2 * T:3 * T], r[0:1, 0:T], r[0:1, 0:T],
                                ALU.mult)
                return e.tensor_tensor(r[0:1, T:2 * T], r[0:1, T:2 * T],
                                       r[0:1, 2 * T:3 * T], ALU.subtract)
            P.add("dve", "c", rows_dve, reads=bufs("p4d"),
                  writes=bufs("rows"), tag=tg + "_ln2_rows")

        def b_rows_act(it):
            tg = f"B{it // TPS}_{it % TPS}"

            def rows_act(e):
                r = rows_t.ap()
                e.activation(r[0:1, 2 * T:3 * T], r[0:1, T:2 * T], AF.Ln,
                             bias=epsc_t.ap()[0:1, 0:1])
                return e.activation(r[0:1, T:2 * T], r[0:1, 2 * T:3 * T],
                                    AF.Exp, scale=-0.5)
            P.add("scalar", "c", rows_act, reads=bufs("rows", "eps"),
                  writes=bufs("rows"), tag=tg + "_ln2_rstd")

        def b_bc(it):
            tg = f"B{it // TPS}_{it % TPS}"
            P.add("pe", "c", lambda e: e.matmul(
                p4.ap()[:, 1536:2048], onesf_t.ap(),
                rows_t.ap()[0:1, 0:2 * T], start=True, stop=True),
                reads=bufs("rows", "ones"), writes=bufs("p4d"),
                tag=tg + "_ln2_bcast")
            P.add("dve", "c", lambda e: (
                e.tensor_copy(bca, p4.ap()[:, 1536:1536 + T]),
                e.tensor_copy(bcb, p4.ap()[:, 1792:1792 + T]))[-1],
                reads=bufs("p4d"), writes=bufs("bca", "bcb"),
                tag=tg + "_bev")

        def b_fin_half(it, half):
            b, ti = it // TPS, it % TPS
            p = it % 2
            par = it % 2
            tg = f"B{b}_{ti}"
            lo, hi = (0, 4) if half == 0 else (4, KD)
            z3 = zfp[p].rearrange("p (k t) -> p k t", k=KD)[:, lo:hi, :]
            x3 = xf[par].rearrange("p (k t) -> p k t", k=KD)[:, lo:hi, :]
            mbv = bca.unsqueeze(1).broadcast_to([128, hi - lo, T])
            rbv = bcb.unsqueeze(1).broadcast_to([128, hi - lo, T])
            zB = (zfBlo if half == 0 else zfBhi)[p]

            def fin(e, z3=z3, x3=x3, mbv=mbv, rbv=rbv):
                e.scalar_tensor_tensor(z3, z3, 0.0, mbv, ALU.bypass,
                                       ALU.subtract)
                e.scalar_tensor_tensor(z3, z3, 0.0, rbv, ALU.bypass,
                                       ALU.mult)
                return e.scalar_tensor_tensor(x3, z3, 0.0, x3, ALU.bypass,
                                              ALU.add)
            P.add("dve", "c", fin,
                  reads=zB + bufs("bca", "bcb") + [xfB[par]],
                  writes=zB + [xfB[par]], tag=tg + f"_fin{half}")

        def b_stx(it):
            b, ti = it // TPS, it % TPS
            par = it % 2
            P.add("sync", "d", lambda e, b=b, t0=ti * T, par=par:
                  e.dma_start(out=xnew_tile(b, t0),
                              in_=xf[par].rearrange("p (k t) -> p k t", k=KD)),
                  reads=[xfB[par]], writes=[B_[f"xnd{it}"]],
                  tag=f"B{b}_{ti}_stx")

        b_ldx(0)
        b_ldx(1)
        for it in range(NTILE + 3):
            i_fin = it - 3
            i_tail = it - 2
            i_zev = it - 1
            if 0 <= i_fin:
                b_fin_half(i_fin, 0)
            if 0 <= i_zev < NTILE:
                b_zevA(i_zev)
            if 0 <= i_fin:
                b_fin_half(i_fin, 1)
                b_stx(i_fin)
                if i_fin + 2 < NTILE:
                    b_ldx(i_fin + 2)
            if 0 <= i_zev < NTILE:
                b_zevB(i_zev)
            if it < NTILE:
                b_omm(it)
                b_oev(it)
                b_wmmA(it)
            if 0 <= i_tail < NTILE:
                b_stats_mm(i_tail)
            if it < NTILE:
                b_wmmB(it)
            if 0 <= i_tail < NTILE:
                b_rows_act(i_tail)
                b_bc(i_tail)
            if 0 <= i_zev < NTILE:
                b_casts(i_zev)

        P.add("dve", "c", lambda e: e.tensor_copy(wfc[0:1, 0:1],
                                                  wfc[0:1, 0:1]),
              writes=bufs("wo") +
              [B_[f"fck{k}q{qt}"] for k in (6, 7)
               for qt in range(4)], tag="fence_wo")
        P.add("dve", "c", lambda e: e.tensor_copy(qs[0:1, 0:1],
                                                  qs[0:1, 0:1]),
              writes=bufs("qs") +
              [B_[f"pj{k}"] for k in range(32)], tag="fence_qs")

    def fin_scales():
        P.add("sync", "d", lambda e: e.dma_start(out=id_ap, in_=id128_d),
              writes=bufs("r2a"), tag="ld_id")
        spv = scpt_t.ap().rearrange("p (j u) -> p j u", u=4)
        P.add("gpsimd", "c", lambda e: (
            e.tensor_tensor(sccp_t.ap().unsqueeze(2), spv[:, :, 0:1],
                            spv[:, :, 1:2], ALU.add),
            e.tensor_tensor(sccp_t.ap().unsqueeze(2), sccp_t.ap().unsqueeze(2),
                            spv[:, :, 2:3], ALU.add),
            e.tensor_tensor(sccp_t.ap().unsqueeze(2), sccp_t.ap().unsqueeze(2),
                            spv[:, :, 3:4], ALU.add))[-1],
            reads=bufs("scpt", "sccp"), writes=bufs("sccp"), tag="scp_comb")
        P.add("gpsimd", "c", lambda e: e.tensor_scalar(
            sccf_t.ap(), sccf_t.ap(), 1.0 / D, 1e-5, ALU.mult, ALU.max),
            reads=bufs("sccf"), writes=bufs("sccf"), tag="scf_clip")
        P.add("gpsimd", "c", lambda e: e.tensor_scalar(
            sccp_t.ap(), sccp_t.ap(), 1.0 / HID, 1e-5, ALU.mult, ALU.max),
            reads=bufs("sccp"), writes=bufs("sccp"), tag="scp_clip")
        P.add("pe", "c", lambda e: e.matmul(
            p4.ap()[0:32, 1536:1664], sccf_t.ap(), id_ap, start=True, stop=True),
            reads=bufs("sccf", "r2a"), writes=bufs("p4d"), tag="scf_tr")
        P.add("pe", "c", lambda e: e.matmul(
            p4.ap()[0:8, 1792:1920], sccp_t.ap(), id_ap, start=True, stop=True),
            reads=bufs("sccp", "r2a"), writes=bufs("p4d"), tag="scp_tr")
        P.add("dve", "c", lambda e: (
            e.tensor_copy(sct_ap[0:32, :], p4.ap()[0:32, 1536:1664]),
            e.tensor_copy(sct_ap[32:40, :], p4.ap()[0:8, 1792:1920]))[-1],
            reads=bufs("p4d", "r2a"), writes=bufs("sct", "r2a"), tag="sc_ev")
        P.add("scalar", "d", lambda e: e.dma_start(
            out=scal_d[0:1, 0:4096].rearrange("o (a c) -> (o a) c", c=128),
            in_=sct_ap[0:32, :]),
            reads=bufs("sct", "r2a"), writes=bufs("scdram1"), tag="sc_st1")
        P.add("scalar", "d", lambda e: e.dma_start(
            out=scal_d[0:1, 4096:5120].rearrange("o (a c) -> (o a) c", c=128),
            in_=sct_ap[32:40, :]),
            reads=bufs("sct", "r2a"), writes=bufs("scdram2"), tag="sc_st2")

    tern_q = [0]

    scb_sel = [0]

    def tern_prep(scal_off, tag, use_prow=False):
        scb_sel[0] ^= 1
        dst, dstB = ((scb, bufs("r2a", "r2b")) if scb_sel[0] == 0
                     else (scb2, bufs("attn")))
        if use_prow:
            slots = [(prow[0], bufs("prow0"), bufs("prow0", "kvsb")),
                     (prow[1], bufs("prow1"), bufs("prow1", "kvsb"))]
        else:
            srow = sreg.ap()[0:1, 0:2048].bitcast(F32)
            slots = [(rows_t.ap()[0:1, 256:768], bufs("rows"), bufs("rows")),
                     (srow, bufs("z1", "z2", "rnf", "rnb"),
                      bufs("z1", "z2", "rnf", "rnb"))]
        which = "scdram1" if scal_off < 4096 else "scdram2"
        for hh in range(2):
            src, srcB, ldw = slots[hh]
            P.add("scalar", "d", lambda e, o=scal_off + 512 * hh, src=src:
                  e.dma_start(out=src, in_=scal_d[0:1, o:o + 512]),
                  reads=bufs(which), writes=ldw,
                  tag=tag + f"_ld{hh}")
        if use_prow:
            pb, pbB = p4.ap()[:, 1536:2048], bufs("p4d")
        else:
            pb, pbB = p2.ap()[:, 0:512], bufs("p2a")
        for hh in range(2):
            src, srcB, ldw = slots[hh]
            P.add("pe", "c", lambda e, src=src, pb=pb: e.matmul(
                pb, onesf_t.ap(),
                src, start=True, stop=True),
                reads=srcB + bufs("ones"), writes=pbB,
                tag=tag + f"_bc{hh}")
            P.add("dve", "c", lambda e, hh=hh, dst=dst, pb=pb: e.tensor_copy(
                dst[:, 512 * hh:512 * hh + 512], pb),
                reads=pbB,
                writes=dstB, tag=tag + f"_ev{hh}")

    _TPAR = None

    def tern_pair(src_fn, dst_fn, dstbuf_fn, k0, tag, post_b):
        scbv, scbB = ((scb, bufs("r2a", "r2b")) if scb_sel[0] == 0
                      else (scb2, bufs("attn")))
        if post_b:
            pars = [(tchain0, tsg0a, bufs("r2c"), bufs("tsga", "r2d")),
                    (awf, tsg0b, bufs("zbf"), bufs("tsgb", "r2d")),
                    (m1f, zfha, bufs("xn"), bufs("zfha", "zf_hi")),
                    (awf1, zfhb, bufs("zf_lo"), bufs("zfhb", "zf_hi"))]
            npar = 4
        else:
            pars = [(tchain0, tsg0, bufs("r2c"), bufs("r2d")),
                    (awf, awf1, bufs("zbf"), bufs("zf_lo"))]
            npar = 2
        rawB = ([B_["xbf0"]], [B_["xbf1"]])
        units = []
        for u in range(2):
            par = tern_q[0] % npar
            pidx = tern_q[0]
            tern_q[0] += 1
            rawt = rawpair[:, 1024 * u:1024 * u + 1024]
            P.add("sync", "d", lambda e, rawt=rawt, u=u: e.dma_start(
                out=rawt, in_=src_fn(u)),
                writes=rawB[u], tag=tag + f"_ld{u}")
            ch, sg_, chB, sgB = pars[par]
            P.add("scalar", "c", lambda e, rawt=rawt, ch=ch, sg_=sg_: (
                e.activation(ch, rawt, AF.Abs),
                e.activation(sg_, rawt, AF.Sign))[-1],
                reads=rawB[u], writes=list(set(chB + sgB)),
                tag=tag + f"_as{u}")
            units.append((ch, sg_, chB, sgB, pidx))
        (c0, s0, c0B, s0B, p0), (c1, s1, c1B, s1B, p1) = units
        def cmp2(e, c0=c0, c1=c1, scbv=scbv):
            e.scalar_tensor_tensor(c0, c0, 2.0, scbv, ALU.mult, ALU.is_gt)
            e.scalar_tensor_tensor(c1, c1, 2.0, scbv, ALU.mult, ALU.is_gt)
            e.scalar_tensor_tensor(c0, c0, 0.0, scbv, ALU.bypass, ALU.mult)
            return e.scalar_tensor_tensor(c1, c1, 0.0, scbv, ALU.bypass,
                                          ALU.mult)
        P.add("dve", "c", cmp2,
              reads=list(set(c0B + c1B)) + scbB,
              writes=list(set(c0B + c1B)), tag=tag + "_cmp2")
        for u, (ch, sg_, chB, sgB, pidx) in enumerate(units):
            if post_b or pidx % 2 == 0:
                P.add("gpsimd", "c", lambda e, ch=ch, sg_=sg_, k=k0 + u:
                      e.tensor_tensor(dst_fn(k), ch, sg_, ALU.mult),
                      reads=chB + sgB, writes=dstbuf_fn(k0 + u),
                      tag=tag + f"_w{u}")
            else:
                P.add("dve", "c", lambda e, ch=ch, sg_=sg_, k=k0 + u:
                      e.scalar_tensor_tensor(dst_fn(k), ch, 0.0, sg_,
                                             ALU.bypass, ALU.mult),
                      reads=chB + sgB, writes=dstbuf_fn(k0 + u),
                      tag=tag + f"_w{u}")

    def fc_dst(k_qt):
        k, qt = k_qt
        return wfc[:, 4096 * k + 1024 * qt: 4096 * k + 1024 * qt + 1024]

    def fc_dstbuf(k_qt):
        k, qt = k_qt
        return [B_[f"fck{k}q{qt}"]]

    def fc_src(qt, k):
        return lambda u: wfct_d.rearrange("(k p) h -> p k h", p=128)[:, k + u, 1024 * qt:1024 * qt + 1024]

    def fc_tern_qt(qt, ks, post_b):
        for k in ks:
            def dst(kk, qt=qt):
                return fc_dst((kk, qt))
            def dbf(kk, qt=qt):
                return fc_dstbuf((kk, qt))
            tern_pair(fc_src(qt, k), dst, dbf, k, f"tf{qt}_{k}", post_b)

    def pj_tern(post_b=True):
        for k2 in range(0, 32, 2):
            tern_pair(
                lambda u, k2=k2: wpjt_d.rearrange("(k p) o -> p k o", p=128)[:, k2 + u, :],
                lambda k: wpj[:, 1024 * k:1024 * k + 1024],
                lambda k: [B_[f"pj{k}"]], k2, f"tp{k2}", post_b)

    def phaseC():
        P.add("pe", "c", lambda e: e.matmul(
            p1a.ap()[0:16, 0:1], onesf_t.ap()[0:1, 0:16], gates_t.ap(),
            start=True, stop=True),
            reads=bufs("gates", "ones"), writes=bufs("p1a"), tag="gate_mm")
        P.add("dve", "c", lambda e: e.tensor_copy(gatev_t.ap(), p1a.ap()[0:16, 0:1]),
              reads=bufs("p1a"), writes=bufs("gatev"), tag="gate_ev")

        def mm_vones(e):
            for k in range(KD):
                i = e.matmul(p1b.ap()[0:16, 0:1],
                             vt_t.ap()[:, 16 * k:16 * k + 16],
                             onesb_t.ap(),
                             start=(k == 0), stop=(k == KD - 1))
            return i
        P.add("pe", "c", mm_vones, reads=bufs("vt", "ones"),
              writes=bufs("p1b"), tag="vones_mm")
        P.add("dve", "c", lambda e: e.tensor_copy(vones_t.ap(),
                                                  p1b.ap()[0:16, 0:1]),
              reads=bufs("p1b"), writes=bufs("vones"), tag="vones_ev")
        P.add("gpsimd", "d", lambda e: e.dma_start(out=ut, in_=ut_d),
              reads=[], writes=bufs("kvsb", "kvs2", "prow0", "prow1"),
              tag="ld_ut")

        tiles = [(b, ti) for b in range(BS) for ti in range(TPS)]

        def c_ldx(idx):
            b, ti = tiles[idx]
            t0 = ti * T
            par = idx % 2
            P.add("sync", "d", lambda e, b=b, t0=t0, par=par: e.dma_start(
                out=xf[par].rearrange("p (k t) -> p k t", k=KD),
                in_=xnew_tile(b, t0)),
                reads=[B_[f"xnd{idx}"]], writes=[xfB[par]],
                tag=f"C{b}_{ti}_ldx")

        c_ldx(0)
        c_ldx(1)

        for qt in range(4):
            tern_prep(1024 * qt, f"tfp{qt}")
            fc_tern_qt(qt, [6], post_b=True)
        tern_prep(4096, "tpp")
        pj_tern(post_b=True)

        def c_ldxb(idx):
            b, ti = tiles[idx]
            t0 = ti * T
            par = idx % 2
            P.add("gpsimd", "d", lambda e, b=b, t0=t0, par=par: e.dma_start(
                out=xbf[par].rearrange("p (k t) -> p k t", k=KD),
                in_=xnew_tile(b, t0)),
                reads=[B_[f"xnd{idx}"]], writes=[xbfB[par]],
                tag=f"C{b}_{ti}_ldxb")

        def c_xsq(idx):
            b, ti = tiles[idx]
            par = idx % 2
            P.add("scalar", "c", lambda e, par=par: e.activation(
                attn, xbf[par], AF.Square),
                reads=[xbfB[par]], writes=bufs("attn"), tag=f"C{b}_{ti}_xsq")

        def c_stats(idx):
            b, ti = tiles[idx]
            par = idx % 2
            tg = f"C{b}_{ti}"
            lnorm_stats(pair_view(16384 + 4096 * par, 28672),
                        [xbfB[par], B_["attn"]], tg + "_ln3", split_bcast=True)

        def c_bcast(idx):
            b, ti = tiles[idx]
            lnorm_bcast(f"C{b}_{ti}_ln3")

        def c_xn(idx):
            b, ti = tiles[idx]
            par = idx % 2
            ln_apply(xf[par], xn, [xfB[par]], bufs("xn"), f"C{b}_{ti}_xn2")

        def c_vx(idx):
            b, ti = tiles[idx]
            par = idx % 2
            tg = f"C{b}_{ti}"
            xb3 = xbf[par].rearrange("p (k t) -> p k t", k=KD)

            def mm_vx(e, xb3=xb3):
                for k in range(KD):
                    i = e.matmul(p1b.ap()[0:16, 0:T],
                                 vt_t.ap()[:, 16 * k:16 * k + 16],
                                 xb3[:, k, :],
                                 start=(k == 0), stop=(k == KD - 1))
                return i
            P.add("pe", "c", mm_vx, reads=[xbfB[par]] + bufs("vt"),
                  writes=bufs("p1b"), tag=tg + "_vx")
            P.add("dve", "c", lambda e: e.tensor_copy(vxs, p1b.ap()[0:16, 0:T]),
                  reads=bufs("p1b"), writes=bufs("rnf"), tag=tg + "_vxe")

        def c_zcorr(idx):
            b, ti = tiles[idx]
            tg = f"C{b}_{ti}"
            vtmp = kspad_t.ap()[0:16, 0:T]

            def corr(e, vtmp=vtmp):
                e.tensor_scalar(vtmp, p1a.ap()[0:16, 0:T],
                                vones_t.ap()[0:16, 0:1], None, ALU.mult)
                e.scalar_tensor_tensor(vxs, vxs, 0.0, vtmp, ALU.bypass,
                                       ALU.subtract)
                return e.scalar_tensor_tensor(z1_ap, vxs, 0.0,
                                              p1a.ap()[0:16, T:2 * T],
                                              ALU.bypass, ALU.mult)
            P.add("dve", "c", corr,
                  reads=bufs("p1a", "rnf", "vones", "kspad"),
                  writes=bufs("z1", "kspad", "rnf"), tag=tg + "_zc")

        def c_z2(idx):
            b, ti = tiles[idx]
            tg = f"C{b}_{ti}"
            P.add("pe", "c", lambda e: e.matmul(
                p1b.ap()[0:16, 0:T], fbt_t.ap(), z1_ap,
                start=True, stop=True),
                reads=bufs("z1", "fbt"), writes=bufs("p1b"), tag=tg + "_z2")
            P.add("dve", "c", lambda e: e.tensor_scalar(
                z2_ap, p1b.ap()[0:16, 0:T], gatev_t.ap()[0:16, 0:1],
                None, ALU.mult),
                reads=bufs("p1b", "gatev"), writes=bufs("z2"), tag=tg + "_z2e")

        def c_fc(idx, g):
            b, ti = tiles[idx]
            tg = f"C{b}_{ti}"
            xn3 = xn.rearrange("p (k t) -> p k t", k=KD)
            bank = g % 2
            pdst = p2.ap()[:, 512 * bank:512 * bank + 512]
            pbu = [B_["p2a"] if bank == 0 else B_["p2b"]]
            rl_dst = zf_lo[:, 0:512] if bank == 0 else zf_hi[:, 0:512]
            rlB = [B_["zf_lo"] if bank == 0 else B_["zf_hi"]]

            def mm_fc(e, g=g, pdst=pdst, xn3=xn3):
                for mi in range(2):
                    e.matmul(pdst[:, 256 * mi:256 * mi + 256],
                             ut[:, 128 * (2 * g + mi):128 * (2 * g + mi) + 128],
                             z2_ap, start=(mi == 0), stop=False)
                for mi in range(2):
                    m = 2 * g + mi
                    for k in range(KD):
                        i = e.matmul(pdst[:, 256 * mi:256 * mi + 256],
                                     wfc[:, 4096 * k + 128 * m:4096 * k + 128 * m + 128],
                                     xn3[:, k, :],
                                     start=False,
                                     stop=(mi == 1 and k == KD - 1))
                return i
            P.add("pe", "c", mm_fc,
                  reads=bufs("xn", "z2", "kvsb", "kvs2") +
                  [B_[f"fck{k}q{g // 4}"] for k in range(KD)],
                  writes=pbu, tag=tg + f"_fc{g}")
            P.add("scalar", "c", lambda e, pdst=pdst, rl_dst=rl_dst:
                  e.activation(rl_dst, pdst, AF.Relu),
                  reads=pbu, writes=rlB, tag=tg + f"_rl{g}")
            P.add("scalar", "c", lambda e, g=g, rl_dst=rl_dst: e.activation(
                r2s[:, 512 * g:512 * g + 512], rl_dst, AF.Square),
                reads=rlB, writes=[B_["r2" + "abcd"[g // 4]]], tag=tg + f"_sq{g}")

        def c_proj(idx):
            b, ti = tiles[idx]
            tg = f"C{b}_{ti}"

            def mm_pj(e):
                for m2 in range(KD):
                    for k2 in range(32):
                        i = e.matmul(p4.ap()[:, 256 * m2:256 * m2 + 256],
                                     wpj[:, 1024 * k2 + 128 * m2:1024 * k2 + 128 * m2 + 128],
                                     r2s[:, 256 * k2:256 * k2 + 256],
                                     start=(k2 == 0), stop=(k2 == 31))
                return i
            P.add("pe", "c", mm_pj,
                  reads=bufs("r2a", "r2b", "r2c", "r2d", "qs") +
                  [B_[f"pj{k}"] for k in range(32)],
                  writes=p4bufs(0, 2048), tag=tg + "_pj")

        def c_tail_ev(idx):
            b, ti = tiles[idx]
            tg = f"C{b}_{ti}"
            P.add("scalar", "c", lambda e: e.activation(zbf, p4.ap()[:, 0:2048], AF.Copy),
                  reads=p4bufs(0, 2048), writes=bufs("zbf"), tag=tg + "_obf")
            P.add("scalar", "c", lambda e: e.activation(attn, p4.ap()[:, 0:2048], AF.Square),
                  reads=p4bufs(0, 2048), writes=bufs("attn"), tag=tg + "_osq")

        def c_tail_stats(idx):
            b, ti = tiles[idx]
            lnorm_stats(pair_view(40960, 28672),
                        bufs("zbf", "attn"), f"C{b}_{ti}_ln4", x_first=False,
                        split_bcast=True)

        def c_tail_fin(idx):
            b, ti = tiles[idx]
            t0 = ti * T
            par = idx % 2
            tg = f"C{b}_{ti}"
            lnorm_bcast(tg + "_ln4")
            P.add("dve", "c", lambda e: (
                e.tensor_copy(bca, p1a.ap()[:, 0:T]),
                e.tensor_copy(bcb, p1a.ap()[:, T:2 * T]))[-1],
                reads=bufs("p1a"), writes=bufs("bca", "bcb"),
                tag=tg + "_bev")
            z3 = zf.rearrange("p (k t) -> p k t", k=KD)
            x3 = xf[par].rearrange("p (k t) -> p k t", k=KD)
            p43 = p4.ap()[:, 0:2048].rearrange("p (k t) -> p k t", k=KD)
            mbv = bca.unsqueeze(1).broadcast_to([128, KD, T])
            rbv = bcb.unsqueeze(1).broadcast_to([128, KD, T])

            def fin(e, z3=z3, x3=x3, p43=p43, mbv=mbv, rbv=rbv):
                e.scalar_tensor_tensor(z3, p43, 0.0, mbv, ALU.bypass,
                                       ALU.subtract)
                e.scalar_tensor_tensor(z3, z3, 0.0, rbv, ALU.bypass,
                                       ALU.mult)
                return e.scalar_tensor_tensor(x3, z3, 0.0, x3, ALU.bypass,
                                              ALU.add)
            P.add("dve", "c", fin,
                  reads=p4bufs(0, 2048) + bufs("bca", "bcb") + [xfB[par]],
                  writes=bufs("zf_lo", "zf_hi") + [xfB[par]], tag=tg + "_fin")
            P.add("gpsimd", "d", lambda e, b=b, t0=t0, par=par: e.dma_start(
                out=out_tile(b, t0),
                in_=xf[par].rearrange("p (k t) -> p k t", k=KD)),
                reads=[xfB[par]] + bufs("odram"), tag=tg + "_sto")

        n = len(tiles)
        c_ldxb(0)
        if n > 1:
            c_ldxb(1)
        c_xsq(0)
        c_stats(0)
        c_vx(0)
        c_bcast(0)
        c_zcorr(0)
        c_xn(0)
        c_z2(0)
        for idx in range(n):
            if idx + 2 < n:
                c_ldxb(idx + 2)
            if idx > 0:
                c_tail_ev(idx - 1)
            for g in range(4):
                c_fc(idx, g)
            if idx > 0:
                c_tail_stats(idx - 1)
            for g in range(4, 7):
                c_fc(idx, g)
            if idx > 0:
                c_tail_fin(idx - 1)
            if idx + 1 < n:
                c_xsq(idx + 1)
            if idx + 1 < n and idx + 1 > 1:
                c_ldx(idx + 1)
            for g in range(7, 11):
                c_fc(idx, g)
            if idx + 1 < n:
                c_stats(idx + 1)
                c_vx(idx + 1)
            for g in range(11, 13):
                c_fc(idx, g)
            if idx + 1 < n:
                c_bcast(idx + 1)
                c_zcorr(idx + 1)
            for g in range(13, 16):
                c_fc(idx, g)
            if idx + 1 < n:
                c_xn(idx + 1)
                c_z2(idx + 1)
            c_proj(idx)
        c_tail_ev(n - 1)
        c_tail_stats(n - 1)
        c_tail_fin(n - 1)

    ph0()
    if "A" in phases:
        phaseA()
    if "B" in phases:
        phaseB()
    if "C" in phases:
        phaseC()

    if debug_taps:
        P.add("scalar", "d", lambda e: e.dma_start(out=dq_d, in_=qs),
              reads=bufs("qs", "odram"), tag="dbg_q")
        P.add("scalar", "d", lambda e: e.dma_start(out=dkv_d, in_=kvs2),
              reads=bufs("kvs2", "odram"), tag="dbg_kv")
        P.add("scalar", "d", lambda e: e.dma_start(out=dks_d, in_=kspad_t.ap()),
              reads=bufs("kspad", "odram"), tag="dbg_ks")
        if "C" in phases:
            P.add("scalar", "d", lambda e: e.dma_start(out=dwf_d, in_=wfc),
                  reads=bufs("wq", "wkv", "wo", "odram") +
                  [B_[f"fck{k}q{qt}"] for k in range(8) for qt in range(4)],
                  tag="dbg_wfc")
            P.add("scalar", "d", lambda e: e.dma_start(out=dwp_d, in_=wpj),
                  reads=bufs("qs", "odram") +
                  [B_[f"pj{k}"] for k in range(32)], tag="dbg_wpj")
            P.add("scalar", "d", lambda e: e.dma_start(out=dsc_d, in_=scal_d),
                  reads=bufs("scdram1", "scdram2", "odram"), tag="dbg_scal")

    P.drain_op = P.add("sync", "c", lambda e: None,
                       writes=bufs("odram"), tag="final_wait")

    from contextlib import ExitStack
    with ExitStack() as stack:
        sems = {}
        for s in ("pe", "dve", "act", "pool", "syncc"):
            sems[(s, 0)] = stack.enter_context(nc.semaphore(f"s_{s}"))
        for q in ("sync", "gdma", "sdma"):
            for j in range(Prog.NSLOT):
                sems[(q, j)] = stack.enter_context(nc.semaphore(f"s_{q}{j}"))
        block = stack.enter_context(nc.Block())
        bodies = P.emit(nc, sems)
        block.tensor(bodies["pe"])
        block.vector(bodies["dve"])
        block.scalar(bodies["scalar"])
        block.gpsimd(bodies["gpsimd"])
        block.sync(bodies["sync"])

    return nc



def _host_inputs(L0_attn_x, L0_attn_w_qkv, L0_attn_w_out, L0_ffn_fast_b,
                 L0_ffn_w_fc, L0_ffn_w_proj, L0_ffn_u, L0_ffn_v, L0_ffn_gate):
    x = np.asarray(L0_attn_x, dtype=np.float32)
    wqkv = np.asarray(L0_attn_w_qkv, dtype=np.float32)
    shared = {
        "wq": np.ascontiguousarray(wqkv[0:D].T),
        "wkv": np.ascontiguousarray(wqkv[D:3 * D].T),
        "wo": np.ascontiguousarray(np.asarray(L0_attn_w_out, np.float32).T),
        "wfcn": np.ascontiguousarray(np.asarray(L0_ffn_w_fc, np.float32)),
        "wfct": np.ascontiguousarray(np.asarray(L0_ffn_w_fc, np.float32).T),
        "wpjn": np.ascontiguousarray(np.asarray(L0_ffn_w_proj, np.float32)),
        "wpjt": np.ascontiguousarray(np.asarray(L0_ffn_w_proj, np.float32).T),
        "vt": np.ascontiguousarray(np.asarray(L0_ffn_v, np.float32).T),
        "fbt": np.ascontiguousarray(np.asarray(L0_ffn_fast_b, np.float32).T),
        "ut": np.ascontiguousarray(np.asarray(L0_ffn_u, np.float32).T),
        "gate": np.asarray(L0_ffn_gate, np.float32).reshape(1, 1),
        "id128": np.eye(128, dtype=np.float32),
    }
    selm = np.zeros((H, 1024), np.float32)
    for c in range(KD):
        selm[2 * c, 128 * c:128 * c + 64] = 1.0
        selm[2 * c + 1, 128 * c + 64:128 * c + 128] = 1.0
    shared["selm"] = selm
    in_maps = []
    for core in range(NCORES):
        m = dict(shared)
        m["x"] = np.ascontiguousarray(
            x[BS * core:BS * (core + 1)].transpose(0, 2, 1))
        in_maps.append(m)
    return in_maps


def kernel(**inputs):
    in_maps = _host_inputs(**inputs)
    nc = build_program()
    res = run_bass_kernel_spmd(nc, in_maps, core_ids=list(range(NCORES)))
    parts = [r["out"].transpose(0, 2, 1) for r in res.results]
    return np.ascontiguousarray(np.concatenate(parts, axis=0)).astype(np.float32)


# version 53
# speedup vs baseline: 1.0197x; 1.0197x over previous
import numpy as np
import ml_dtypes
import concourse.bass as bass
import concourse.mybir as mybir
from concourse.bass import AP
from concourse.bass_utils import run_bass_kernel_spmd

F32 = mybir.dt.float32
BF16 = mybir.dt.bfloat16
U8 = mybir.dt.uint8
AF = mybir.ActivationFunctionType
ALU = mybir.AluOpType
AX = mybir.AxisListType

NCORES = 8
ELU_PARTS = "aerx"
BS = 2
S = 2048
D = 1024
H = 16
HD = 64
HID = 4096
RK = 16
T = 256
TPS = S // T
NTILE = BS * TPS
KD = D // 128


_STREAMS = ("pe", "dve", "scalar", "gpsimd", "sync")
_SEM_OF = {("pe", "c"): "pe", ("dve", "c"): "dve", ("scalar", "c"): "act",
           ("scalar", "d"): "sdma", ("gpsimd", "c"): "pool",
           ("gpsimd", "d"): "gdma", ("sync", "d"): "sync", ("sync", "c"): "syncc"}


class Buf:
    __slots__ = ("name", "writers", "readers", "exclusive")

    def __init__(self, name, exclusive=False):
        self.name = name
        self.writers = []
        self.readers = []
        self.exclusive = exclusive


class Prog:
    def __init__(self):
        self.ops = []
        self.stream_ops = {s: [] for s in _STREAMS}
        self.drain_op = None

    def add(self, stream, kind, fn, reads=(), writes=(), deps=(), tag=""):
        i = len(self.ops)
        dep = set(deps)
        for b in reads:
            dep.update(b.writers)
            if b.exclusive:
                dep.update(b.readers)
        for b in writes:
            dep.update(b.writers)
            dep.update(b.readers)
        dep.discard(i)
        op = dict(id=i, stream=stream, kind=kind, fn=fn,
                  deps=sorted(dep), tag=tag)
        self.ops.append(op)
        self.stream_ops[stream].append(i)
        for b in reads:
            b.readers.append(i)
        for b in writes:
            b.writers = [i]
            b.readers = []
        return i

    NSLOT = 8

    def emit(self, nc, sems):
        ops = self.ops
        needed = set()
        for op in ops:
            for d in op["deps"]:
                if ops[d]["stream"] != op["stream"]:
                    needed.add(d)
        counters = {}
        dma_idx = {}
        inc_val = {}
        self_wait = {}
        for op in ops:
            sem = _SEM_OF[(op["stream"], op["kind"])]
            if op["kind"] == "d":
                i = dma_idx.get(sem, 0)
                dma_idx[sem] = i + 1
                key = (sem, i % self.NSLOT)
                occ = i // self.NSLOT
                inc_val[op["id"]] = (key, 16 * (occ + 1))
                if occ > 0:
                    self_wait[op["id"]] = (key, 16 * occ)
            elif op["id"] in needed:
                counters[sem] = counters.get(sem, 0) + 1
                inc_val[op["id"]] = ((sem, 0), counters[sem])
        self._inc_val = inc_val

        def make_body(stream):
            def body(eng):
                waited = {}
                for i in self.stream_ops[stream]:
                    op = ops[i]
                    need = {}
                    for d in op["deps"]:
                        if ops[d]["stream"] == stream:
                            if ops[d]["kind"] == "c" or op["kind"] == "d":
                                continue
                        key, dval = inc_val[ops[d]["id"]]
                        if need.get(key, 0) < dval:
                            need[key] = dval
                    if i in self_wait:
                        key, dval = self_wait[i]
                        if need.get(key, 0) < dval:
                            need[key] = dval
                    if i == self.drain_op:
                        for oid, (key, dval) in inc_val.items():
                            if ops[oid]["kind"] == "d" and need.get(key, 0) < dval:
                                need[key] = dval
                    for key, dval in sorted(need.items()):
                        if waited.get(key, 0) < dval:
                            eng.wait_ge(sems[key], dval)
                            waited[key] = dval
                    inst = op["fn"](eng)
                    if i in inc_val:
                        assert inst is not None, f"op {i} ({op['tag']}) marked but returned no instruction"
                        key, _ = inc_val[i]
                        step = 16 if op["kind"] == "d" else 1
                        inst.then_inc(sems[key], step)
            return body
        return {s: make_body(s) for s in _STREAMS}



def build_program(phases=("A", "B", "C"), debug_taps=False,
                  a_seqs=BS, a_tiles=TPS, a_asm=True, a_stage=9):
    nc = bass.Bass("TRN2", target_bir_lowering=False,
                   detect_race_conditions=False)

    x_d = nc.dram_tensor("x", [BS, D, S], F32, kind="ExternalInput").ap()
    wq_d = nc.dram_tensor("wq", [D, D], F32, kind="ExternalInput").ap()
    wkv_d = nc.dram_tensor("wkv", [D, 2 * D], F32, kind="ExternalInput").ap()
    wo_d = nc.dram_tensor("wo", [D, D], F32, kind="ExternalInput").ap()
    wfcn_d = nc.dram_tensor("wfcn", [HID, D], F32, kind="ExternalInput").ap()
    wfct_d = nc.dram_tensor("wfct", [D, HID], F32, kind="ExternalInput").ap()
    wpjn_d = nc.dram_tensor("wpjn", [D, HID], F32, kind="ExternalInput").ap()
    wpjt_d = nc.dram_tensor("wpjt", [HID, D], F32, kind="ExternalInput").ap()
    vt_d = nc.dram_tensor("vt", [D, RK], F32, kind="ExternalInput").ap()
    fbt_d = nc.dram_tensor("fbt", [RK, RK], F32, kind="ExternalInput").ap()
    ut_d = nc.dram_tensor("ut", [RK, HID], F32, kind="ExternalInput").ap()
    gate_d = nc.dram_tensor("gate", [1, 1], F32, kind="ExternalInput").ap()
    selm_d = nc.dram_tensor("selm", [H, 1024], F32, kind="ExternalInput").ap()
    id128_d = nc.dram_tensor("id128", [128, 128], F32, kind="ExternalInput").ap()
    out_d = nc.dram_tensor("out", [BS, D, S], F32, kind="ExternalOutput").ap()
    xnew_kind = "ExternalOutput" if debug_taps else "Internal"
    xnew_d = nc.dram_tensor("xnew", [BS, D, S], F32, kind=xnew_kind).ap()
    scal_d = nc.dram_tensor("scal", [1, 5120], F32, kind="Internal").ap()
    if debug_taps:
        dq_d = nc.dram_tensor("dbg_q", [128, KD * BS * S], BF16, kind="ExternalOutput").ap()
        dkv_d = nc.dram_tensor("dbg_kvs", [128, 2048], BF16, kind="ExternalOutput").ap()
        dks_d = nc.dram_tensor("dbg_ksp", [128, 256], BF16, kind="ExternalOutput").ap()
        dwf_d = nc.dram_tensor("dbg_wfc", [128, KD * HID], BF16, kind="ExternalOutput").ap()
        dwp_d = nc.dram_tensor("dbg_wpj", [128, 32 * 1024], BF16, kind="ExternalOutput").ap()
        dsc_d = nc.dram_tensor("dbg_scal", [1, 5120], F32, kind="ExternalOutput").ap()

    r1 = nc.alloc_sbuf_tensor("r1", [128, 131072], U8)

    def carve(region, byte_off, nbytes, dtype, cols):
        apx = region.ap()[:, byte_off:byte_off + nbytes].bitcast(dtype)
        assert apx.shape[1] == cols, (apx.shape, cols)
        return apx

    wq = carve(r1, 0, 16384, BF16, KD * 1024)
    wkv = carve(r1, 16384, 32768, BF16, KD * 2048)
    wo = carve(r1, 49152, 16384, BF16, KD * 1024)
    qs = carve(r1, 65536, 65536, BF16, KD * 4096)
    wfc = carve(r1, 0, 65536, BF16, KD * 4096)
    wpj = carve(r1, 65536, 65536, BF16, 32 * 1024)

    rs = nc.alloc_sbuf_tensor("rs", [128, 47104], U8)
    xf = [carve(rs, 0, 8192, F32, 2048), carve(rs, 8192, 8192, F32, 2048)]
    xbf = [carve(rs, 16384, 4096, BF16, 2048), carve(rs, 20480, 4096, BF16, 2048)]
    xn = carve(rs, 24576, 4096, BF16, 2048)
    attn = carve(rs, 28672, 4096, BF16, 2048)
    zf = carve(rs, 32768, 8192, F32, 2048)
    zf_lo = carve(rs, 32768, 4096, F32, 1024)
    zf_hi = carve(rs, 36864, 4096, F32, 1024)
    zbf = carve(rs, 40960, 4096, BF16, 2048)
    bca = carve(rs, 45056, 1024, F32, 256)
    bcb = carve(rs, 46080, 1024, F32, 256)
    rs_bf = rs.ap().bitcast(BF16)
    RSROW = rs_bf.ap[0][0]

    def pair_view(x_byte_off, sq_byte_off):
        def f(k, x_byte_off=x_byte_off, sq_byte_off=sq_byte_off):
            lo = min(x_byte_off, sq_byte_off) // 2
            hi = max(x_byte_off, sq_byte_off) // 2
            return AP(rs_bf.tensor, lo + T * k,
                      [[RSROW, 128], [hi - lo, 2], [1, T]])
        return f

    rposf = carve(rs, 40960, 4096, F32, 1024)
    raw0 = carve(rs, 16384, 4096, F32, 1024)
    raw1 = carve(rs, 20480, 4096, F32, 1024)
    awf = carve(rs, 40960, 4096, F32, 1024)
    sgf = carve(rs, 36864, 4096, F32, 1024)
    m1f = carve(rs, 24576, 4096, F32, 1024)
    msf = carve(rs, 28672, 4096, F32, 1024)
    awf1 = carve(rs, 32768, 4096, F32, 1024)
    sgf1 = carve(rs, 0, 4096, F32, 1024)
    m1f1 = carve(rs, 4096, 4096, F32, 1024)
    msf1 = carve(rs, 8192, 4096, F32, 1024)

    r2 = nc.alloc_sbuf_tensor("r2", [128, 16384], U8)
    kbf = carve(r2, 0, 2048, BF16, 1024)
    vbf = carve(r2, 2048, 2048, BF16, 1024)
    wnat0a = carve(r2, 4096, 4096, F32, 1024)
    wnat0b = carve(r2, 8192, 4096, F32, 1024)
    wnat1q = carve(r2, 12288, 4096, F32, 1024)
    r2s = carve(r2, 0, 16384, BF16, 8192)

    rk = nc.alloc_sbuf_tensor("rk", [128, 8320], U8)
    kv_sb = rk.ap()[0:64, 0:4096].bitcast(F32)
    ksum_sb = rk.ap()[0:64, 4096:4160].bitcast(F32)
    prow = [rk.ap()[0:1, 0:2048].bitcast(F32),
            rk.ap()[0:1, 2048:4096].bitcast(F32)]
    kvs2 = rk.ap()[:, 4224:8320].bitcast(BF16)
    ut = rk.ap()[0:16, 0:8192].bitcast(BF16)

    kspad_t = nc.alloc_sbuf_tensor("kspad", [128, 256], BF16)
    selm_t = nc.alloc_sbuf_tensor("selm_s", [16, 1024], BF16)
    rows_t = nc.alloc_sbuf_tensor("rows", [1, 768], F32)
    sreg = nc.alloc_sbuf_tensor("sreg", [16, 2048], U8)
    z1_ap = sreg.ap()[0:16, 0:512].bitcast(BF16)
    z2_ap = sreg.ap()[0:16, 512:1024].bitcast(BF16)
    rnf_ap = sreg.ap()[0:16, 1024:2048].bitcast(F32)
    rnb2_ap = sreg.ap()[0:16, 512:1024].bitcast(BF16)
    rnb_ap = sreg.ap()[0:16, 0:512].bitcast(BF16)
    vt_t = nc.alloc_sbuf_tensor("vt_s", [128, 128], BF16)
    fbt_t = nc.alloc_sbuf_tensor("fbt_s", [16, 16], BF16)
    gatev_t = nc.alloc_sbuf_tensor("gatev", [16, 1], F32)
    vones_t = nc.alloc_sbuf_tensor("vones", [16, 1], F32)
    vxs = sreg.ap()[0:16, 1024:2048].bitcast(F32)
    gates_t = nc.alloc_sbuf_tensor("gates", [1, 1], F32)
    onesb_t = nc.alloc_sbuf_tensor("onesb", [128, 1], BF16)
    onesf_t = nc.alloc_sbuf_tensor("onesf", [1, 128], F32)
    epsc_t = nc.alloc_sbuf_tensor("epsc", [128, 1], F32)
    sccf_t = nc.alloc_sbuf_tensor("sccf", [128, 32], F32)
    sccp_t = nc.alloc_sbuf_tensor("sccp", [128, 8], F32)
    scpt_t = nc.alloc_sbuf_tensor("scpt", [128, 32], F32)
    sct_ap = r2.ap()[0:40, 0:512].bitcast(F32)
    id_ap = r2.ap()[:, 512:1024].bitcast(F32)
    scb = r2.ap()[:, 1024:5120].bitcast(F32)
    scb2 = rs.ap()[:, 28672:32768].bitcast(F32)
    tchain0 = r2.ap()[:, 8192:12288].bitcast(F32)
    tsg0 = r2.ap()[:, 12288:16384].bitcast(F32)
    zf2 = r2.ap()[:, 8192:16384].bitcast(F32)
    rawpair = rs.ap()[:, 16384:24576].bitcast(F32)
    tsg0a = r2.ap()[:, 12288:14336].bitcast(mybir.dt.bfloat16)
    tsg0b = r2.ap()[:, 14336:16384].bitcast(mybir.dt.bfloat16)
    zfha = rs.ap()[:, 36864:38912].bitcast(mybir.dt.bfloat16)
    zfhb = rs.ap()[:, 38912:40960].bitcast(mybir.dt.bfloat16)
    rawpair2 = rs.ap()[:, 32768:40960].bitcast(F32)
    tch2 = carve(rs, 8192, 4096, F32, 1024)
    tsg2 = carve(rs, 12288, 4096, F32, 1024)
    tch3 = carve(rs, 28672, 4096, F32, 1024)
    tsg3 = carve(rs, 24576, 4096, F32, 1024)

    p4 = nc.alloc_psum_tensor("p4", [128, 2048], F32)
    p2 = nc.alloc_psum_tensor("p2", [128, 1024], F32)
    p1a = nc.alloc_psum_tensor("p1a", [128, 512], F32)
    p1b = nc.alloc_psum_tensor("p1b", [128, 512], F32)

    P = Prog()
    B_ = {}
    for nm in ("wq", "wkv", "wo", "qs", "xf0", "xf1", "xbf0", "xbf1", "xn",
               "attn", "zf_lo", "zf_hi", "zbf", "bca", "bcb",
               "r2a", "r2b", "r2c", "r2d", "kvsb", "kvs2", "kspad", "selm",
               "rows", "rnf", "rnb", "vt", "fbt", "z1", "z2", "gatev",
               "vones",
               "gates", "ones", "eps", "sccf", "sccp", "scpt", "sct", "id",
               "p4a", "p4b", "p4c", "p4d", "p2a", "p2b", "p1a", "p1b",
               "xdram", "odram", "scdram1", "scdram2", "prow0", "prow1",
               "tsga", "tsgb", "zfha", "zfhb",
               *[f"pj{k}" for k in range(32)],
               *[f"fck{k}q{qt}" for k in range(8) for qt in range(4)],
               *[f"xnd{i}" for i in range(16)],
               *[f"qst{i}" for i in range(16)]):
        B_[nm] = Buf(nm, exclusive=nm.startswith(("p4", "p2", "p1")))
    bufs = lambda *names: [B_[n] for n in names]
    xfB = [B_["xf0"], B_["xf1"]]
    xbfB = [B_["xbf0"], B_["xbf1"]]

    def p4bufs(c0, c1):
        out = []
        for j in range(4):
            if c0 < (j + 1) * 512 and c1 > j * 512:
                out.append(B_["p4" + "abcd"[j]])
        return out

    def p2bufs(c0, c1):
        out = []
        if c0 < 512:
            out.append(B_["p2a"])
        if c1 > 512:
            out.append(B_["p2b"])
        return out

    def x_tile(b, t0):
        return x_d.rearrange("b (k p) s -> p b k s", p=128)[:, b, :, t0:t0 + T]

    def xnew_tile(b, t0):
        return xnew_d.rearrange("b (k p) s -> p b k s", p=128)[:, b, :, t0:t0 + T]

    def out_tile(b, t0):
        return out_d.rearrange("b (k p) s -> p b k s", p=128)[:, b, :, t0:t0 + T]

    sems = {}

    def ph0():
        P.add("dve", "c", lambda e: e.memset(onesb_t.ap(), 1.0),
              writes=bufs("ones"), tag="memset_onesb")
        P.add("dve", "c", lambda e: e.memset(onesf_t.ap(), 1.0),
              writes=bufs("ones"), tag="memset_onesf")
        P.add("dve", "c", lambda e: e.memset(epsc_t.ap(), 1e-5),
              writes=bufs("eps"), tag="memset_eps")
        P.add("sync", "d", lambda e: e.dma_start(out=gates_t.ap(), in_=gate_d),
              writes=bufs("gates"), tag="ld_gate")
        P.add("gpsimd", "d", lambda e: e.dma_start(
            out=wq.rearrange("p (k o) -> p k o", k=KD),
            in_=wq_d.rearrange("(k p) o -> p k o", p=128)),
            writes=bufs("wq"), tag="ld_wq")
        P.add("gpsimd", "d", lambda e: e.dma_start(
            out=wkv.rearrange("p (k o) -> p k o", k=KD),
            in_=wkv_d.rearrange("(k p) o -> p k o", p=128)),
            writes=bufs("wkv"), tag="ld_wkv")
        P.add("gpsimd", "d", lambda e: e.dma_start(
            out=wo.rearrange("p (k o) -> p k o", k=KD),
            in_=wo_d.rearrange("(k p) o -> p k o", p=128)),
            writes=bufs("wo"), tag="ld_wo")
        P.add("gpsimd", "d", lambda e: e.dma_start(
            out=vt_t.ap().rearrange("p (k r) -> p k r", k=KD),
            in_=vt_d.rearrange("(k p) r -> p k r", p=128)),
            writes=bufs("vt"), tag="ld_vt")
        P.add("gpsimd", "d", lambda e: e.dma_start(out=fbt_t.ap(), in_=fbt_d),
              writes=bufs("fbt"), tag="ld_fbt")
        P.add("gpsimd", "d", lambda e: e.dma_start(out=selm_t.ap(), in_=selm_d),
              writes=bufs("selm"), tag="ld_selm")

    def lnorm_bcast(tag):
        def mm_bcast(e):
            return e.matmul(p1a.ap()[:, 0:2 * T], onesf_t.ap(),
                            rows_t.ap()[0:1, 0:2 * T], start=True, stop=True)
        P.add("pe", "c", mm_bcast, reads=bufs("rows", "ones"),
              writes=bufs("p1a"), tag=tag + "_bcast")

    def lnorm_stats(pair_rhs_fn, dep_bufs, tag, x_first=True,
                    split_bcast=False):
        xo, so = (0, T) if x_first else (T, 0)

        def mm_stats(e):
            for k in range(KD):
                i = e.matmul(p1a.ap()[0:1, 0:2 * T], onesb_t.ap(),
                             pair_rhs_fn(k),
                             start=(k == 0), stop=(k == KD - 1))
            return i
        P.add("pe", "c", mm_stats, reads=dep_bufs + bufs("ones"),
              writes=bufs("p1a"), tag=tag + "_mm")

        def rows_dve(e):
            r = rows_t.ap()
            e.tensor_scalar(r[0:1, 0:T], p1a.ap()[0:1, xo:xo + T],
                            1.0 / D, None, ALU.mult)
            e.tensor_scalar(r[0:1, T:2 * T], p1a.ap()[0:1, so:so + T],
                            1.0 / D, None, ALU.mult)
            e.tensor_tensor(r[0:1, 2 * T:3 * T], r[0:1, 0:T], r[0:1, 0:T],
                            ALU.mult)
            return e.tensor_tensor(r[0:1, T:2 * T], r[0:1, T:2 * T],
                                   r[0:1, 2 * T:3 * T], ALU.subtract)
        P.add("dve", "c", rows_dve, reads=bufs("p1a"),
              writes=bufs("rows"), tag=tag + "_rows")

        def rows_act(e):
            r = rows_t.ap()
            e.activation(r[0:1, 2 * T:3 * T], r[0:1, T:2 * T], AF.Ln,
                         bias=epsc_t.ap()[0:1, 0:1])
            return e.activation(r[0:1, T:2 * T], r[0:1, 2 * T:3 * T], AF.Exp,
                                scale=-0.5)
        P.add("scalar", "c", rows_act, reads=bufs("rows", "eps"),
              writes=bufs("rows"), tag=tag + "_rstd")

        if not split_bcast:
            lnorm_bcast(tag)

    def ln_apply(xf_ap, out_ap, xf_bufs, out_bufs, tag):
        x3 = xf_ap.rearrange("p (k t) -> p k t", k=KD)
        z3 = zf.rearrange("p (k t) -> p k t", k=KD)
        o3 = out_ap.rearrange("p (k t) -> p k t", k=KD)
        mb = p1a.ap()[:, 0:T].unsqueeze(1).broadcast_to([128, KD, T])
        rb = p1a.ap()[:, T:2 * T].unsqueeze(1).broadcast_to([128, KD, T])

        def f(e):
            e.scalar_tensor_tensor(z3, x3, 0.0, mb, ALU.bypass, ALU.subtract)
            return e.scalar_tensor_tensor(o3, z3, 0.0, rb, ALU.bypass,
                                          ALU.mult)
        P.add("dve", "c", f,
              reads=xf_bufs + bufs("p1a"),
              writes=out_bufs + bufs("zf_lo", "zf_hi"), tag=tag)

    def elu1(psum_ap, psum_bufs, out_ap, out_bufs, width, tag):
        if "a" in ELU_PARTS:
            def a1(e):
                i = e.activation(zf_lo[:, 0:width], psum_ap, AF.Relu, scale=-1.0)
                if "e" in ELU_PARTS:
                    i = e.activation(zf_hi[:, 0:width], zf_lo[:, 0:width],
                                     AF.Exp, scale=-1.0)
                return i
            P.add("scalar", "c", a1, reads=psum_bufs,
                  writes=bufs("zf_lo", "zf_hi"), tag=tag + "_exp")
        if "r" in ELU_PARTS:
            def d1(e):
                return e.scalar_tensor_tensor(out_ap, psum_ap, 0.0,
                                              zf_hi[:, 0:width], ALU.max, ALU.add)
            P.add("dve", "c", d1, reads=psum_bufs + bufs("zf_hi"),
                  writes=out_bufs, tag=tag + "_relu_add")

    def phaseA():
        tiles = [(b, ti) for b in range(a_seqs) for ti in range(a_tiles)]

        def a_load(idx):
            b, ti = tiles[idx]
            t0 = ti * T
            par = idx % 2
            tg = f"A{b}_{ti}"
            P.add("sync", "d", lambda e, b=b, t0=t0, par=par: e.dma_start(
                out=xf[par].rearrange("p (k t) -> p k t", k=KD),
                in_=x_tile(b, t0)), writes=[xfB[par]], tag=tg + "_ldx")
            P.add("gpsimd", "d", lambda e, b=b, t0=t0, par=par: e.dma_start(
                out=xbf[par].rearrange("p (k t) -> p k t", k=KD),
                in_=x_tile(b, t0)), writes=[xbfB[par]], tag=tg + "_ldxb")

        def a_xsq(idx):
            b, ti = tiles[idx]
            par = idx % 2
            P.add("scalar", "c", lambda e, par=par: e.activation(
                attn, xf[par], AF.Square),
                reads=[xfB[par]], writes=bufs("attn"), tag=f"A{b}_{ti}_xsq")

        def a_stats(idx):
            b, ti = tiles[idx]
            par = idx % 2
            tg = f"A{b}_{ti}"
            lnorm_stats(pair_view(16384 + 4096 * par, 28672),
                        [xbfB[par], B_["attn"]], tg + "_ln1", split_bcast=True)

        def a_bcast(idx):
            b, ti = tiles[idx]
            lnorm_bcast(f"A{b}_{ti}_ln1")

        def a_xn(idx):
            b, ti = tiles[idx]
            par = idx % 2
            ln_apply(xf[par], xn, [xfB[par]], bufs("xn"), f"A{b}_{ti}_xn")

        def a_q(idx):
            b, ti = tiles[idx]
            t0 = ti * T
            tg = f"A{b}_{ti}"
            xn3 = xn.rearrange("p (k t) -> p k t", k=KD)
            for g in range(2):
                cols0 = 1024 * g

                def mm_q(e, g=g, cols0=cols0, xn3=xn3):
                    for mi in range(4):
                        m = 4 * g + mi
                        for k in range(KD):
                            i = e.matmul(
                                p4.ap()[:, cols0 + 256 * mi: cols0 + 256 * mi + 256],
                                wq[:, 1024 * k + 128 * m: 1024 * k + 128 * m + 128],
                                xn3[:, k, :],
                                start=(mi % 2 == 0 and k == 0),
                                stop=(mi % 2 == 1 and k == KD - 1))
                    return i
                pb = p4bufs(cols0, cols0 + 1024)
                P.add("pe", "c", mm_q, reads=bufs("xn", "wq"), writes=pb,
                      tag=tg + f"_qmm{g}")
                qdst = AP(qs.tensor, qs.offset + 4096 * 4 * g + 2048 * b + t0,
                          [[qs.ap[0][0], 128], [4096, 4], [1, T]])
                elu1(p4.ap()[:, cols0:cols0 + 1024], pb,
                     qdst, bufs("qs", f"qst{b * TPS + ti}"), 1024,
                     tg + f"_qel{g}")

        def a_kv_mm(idx, g2):
            b, ti = tiles[idx]
            tg = f"A{b}_{ti}"
            xn3 = xn.rearrange("p (k t) -> p k t", k=KD)

            def mm_k(e, g2=g2, xn3=xn3):
                for n2 in range(2):
                    for k in range(KD):
                        i = e.matmul(
                            p2.ap()[:, 512 * n2:512 * n2 + 512],
                            xn3[:, k, 128 * g2:128 * g2 + 128],
                            wkv[:, 2048 * k + 512 * n2: 2048 * k + 512 * n2 + 512],
                            start=(k == 0), stop=(k == KD - 1))
                return i
            P.add("pe", "c", mm_k, reads=bufs("xn", "wkv"),
                  writes=bufs("p2a", "p2b"), tag=tg + f"_kmm{g2}")

        def a_kv_kel(idx, g2):
            b, ti = tiles[idx]
            elu1(p2.ap()[:, 0:1024], bufs("p2a", "p2b"),
                 kbf, bufs("r2a"), 1024, f"A{b}_{ti}_kel{g2}")

        def a_kv_vm(idx, g2, hook=None):
            b, ti = tiles[idx]
            tg = f"A{b}_{ti}"
            xn3 = xn.rearrange("p (k t) -> p k t", k=KD)

            def mm_v(e, g2=g2, xn3=xn3):
                for n2 in range(2):
                    for k in range(KD):
                        i = e.matmul(
                            p4.ap()[:, 512 * n2:512 * n2 + 512],
                            xn3[:, k, 128 * g2:128 * g2 + 128],
                            wkv[:, 2048 * k + 1024 + 512 * n2: 2048 * k + 1024 + 512 * n2 + 512],
                            start=(k == 0), stop=(k == KD - 1))
                return i
            P.add("pe", "c", mm_v, reads=bufs("xn", "wkv"),
                  writes=p4bufs(0, 1024), tag=tg + f"_vmm{g2}")
            P.add("scalar", "c", lambda e: e.activation(
                vbf[:, 0:512], p4.ap()[:, 0:512], AF.Copy),
                reads=p4bufs(0, 512), writes=bufs("r2b"),
                tag=tg + f"_vcpA{g2}")
            P.add("scalar", "c", lambda e: e.activation(
                vbf[:, 512:1024], p4.ap()[:, 512:1024], AF.Copy),
                reads=p4bufs(512, 1024), writes=bufs("r2b"),
                tag=tg + f"_vcpB{g2}")
            if hook is not None:
                hook()

        def a_kv_upd(idx, g2):
            b, ti = tiles[idx]
            tg = f"A{b}_{ti}"

            def mm_kv(e):
                for h in range(H):
                    i = e.matmul(
                        p4.ap()[0:64, 1024 + 64 * h: 1024 + 64 * h + 64],
                        kbf[:, 64 * h:64 * h + 64],
                        vbf[:, 64 * h:64 * h + 64],
                        start=(h % 8 == 0), stop=(h % 8 == 7))
                return i
            P.add("pe", "c", mm_kv, reads=bufs("r2a", "r2b"),
                  writes=p4bufs(1024, 2048), tag=tg + f"_kvmm{g2}")

            def mm_ks(e):
                for h in range(H):
                    i = e.matmul(
                        p1b.ap()[0:64, h:h + 1],
                        kbf[:, 64 * h:64 * h + 64],
                        onesb_t.ap(),
                        start=(h == 0), stop=(h == H - 1))
                return i
            P.add("pe", "c", mm_ks, reads=bufs("r2a", "ones"),
                  writes=bufs("p1b"), tag=tg + f"_ksmm{g2}")

            def acc(e):
                e.tensor_tensor(kv_sb, kv_sb, p4.ap()[0:64, 1024:2048],
                                ALU.add)
                return e.tensor_tensor(ksum_sb, ksum_sb,
                                       p1b.ap()[0:64, 0:16], ALU.add)
            P.add("dve", "c", acc,
                  reads=p4bufs(1024, 2048) + bufs("p1b", "kvsb"),
                  writes=bufs("kvsb"), tag=tg + f"_acc{g2}")

        def a_seqend(b):
            if not a_asm:
                return
            P.add("dve", "c", lambda e, b=b: e.memset(
                kvs2[:, 1024 * b:1024 * b + 1024], 0.0),
                writes=bufs("kvs2"), tag=f"A{b}_kvz")
            P.add("dve", "c", lambda e, b=b: e.memset(
                kspad_t.ap()[:, 128 * b:128 * b + 128], 0.0),
                writes=bufs("kspad"), tag=f"A{b}_ksz")

            def kv_even(e, b=b):
                src = AP(kv_sb.tensor, kv_sb.offset,
                         [[kv_sb.ap[0][0], 64], [128, KD], [1, 64]])
                dst = AP(kvs2.tensor, kvs2.offset + 1024 * b,
                         [[kvs2.ap[0][0], 64], [128, KD], [1, 64]])
                return e.dma_start(out=dst, in_=src)
            P.add("gpsimd", "d", kv_even, reads=bufs("kvsb"),
                  writes=bufs("kvs2"), tag=f"A{b}_kve")

            def kv_odd(e, b=b):
                src = AP(kv_sb.tensor, kv_sb.offset + 64,
                         [[kv_sb.ap[0][0], 64], [128, KD], [1, 64]])
                t = kvs2[64:128, 1024 * b:1024 * b + 1024]
                dst = AP(t.tensor, t.offset + 64,
                         [[t.ap[0][0], 64], [128, KD], [1, 64]])
                return e.dma_start(out=dst, in_=src)
            P.add("gpsimd", "d", kv_odd, reads=bufs("kvsb"),
                  writes=bufs("kvs2"), tag=f"A{b}_kvo")

            def ks_even(e, b=b):
                src = AP(ksum_sb.tensor, ksum_sb.offset,
                         [[ksum_sb.ap[0][0], 64], [2, KD], [1, 1]])
                t = kspad_t.ap()[0:64, 128 * b:128 * b + 128]
                dst = AP(t.tensor, t.offset, [[t.ap[0][0], 64], [18, KD], [1, 1]])
                return e.dma_start(out=dst, in_=src,
                                   allow_slow_non_contiguous=True)
            P.add("gpsimd", "d", ks_even, reads=bufs("kvsb"),
                  writes=bufs("kspad"), tag=f"A{b}_kse")

            def ks_odd(e, b=b):
                src = AP(ksum_sb.tensor, ksum_sb.offset + 1,
                         [[ksum_sb.ap[0][0], 64], [2, KD], [1, 1]])
                t = kspad_t.ap()[64:128, 128 * b:128 * b + 128]
                dst = AP(t.tensor, t.offset + 1,
                         [[t.ap[0][0], 64], [18, KD], [1, 1]])
                return e.dma_start(out=dst, in_=src,
                                   allow_slow_non_contiguous=True)
            P.add("gpsimd", "d", ks_odd, reads=bufs("kvsb"),
                  writes=bufs("kspad"), tag=f"A{b}_kso")

        def a_scale(it):
            tg = f"As{it}"
            for s in range(2):
                c0 = 2 * it + s
                wn = wnat0a if s == 0 else wnat0b
                wnB = bufs("r2b") if s == 0 else bufs("r2c")
                P.add("sync", "d", lambda e, c0=c0, wn=wn: e.dma_start(
                    out=wn,
                    in_=wfcn_d.rearrange("(c p) i -> p c i", p=128)[:, c0, :]),
                    writes=wnB, tag=tg + f"_ldn{s}")
                P.add("dve", "c", lambda e, c0=c0, wn=wn: e.tensor_reduce(
                    sccf_t.ap()[:, c0:c0 + 1], wn, AX.X, ALU.add,
                    apply_absolute_value=True),
                    reads=wnB, writes=bufs("sccf"), tag=tg + f"_red{s}")
            for s in range(2):
                qq = 2 * it + s
                j, u = qq // 4, qq % 4
                P.add("sync", "d", lambda e, j=j, u=u: e.dma_start(
                    out=wnat1q,
                    in_=wpjn_d.rearrange("(c p) (u q) -> p c u q",
                                         p=128, q=1024)[:, j, u, :]),
                    writes=bufs("r2d"), tag=tg + f"_ldp{s}")
                P.add("dve", "c", lambda e, qq=qq: e.tensor_reduce(
                    scpt_t.ap()[:, qq:qq + 1], wnat1q, AX.X, ALU.add,
                    apply_absolute_value=True),
                    reads=bufs("r2d"), writes=bufs("scpt"), tag=tg + f"_rdp{s}")

        n = len(tiles)
        if n:
            a_load(0)
            if n > 1:
                a_load(1)
            a_xsq(0)
            a_stats(0)
            a_bcast(0)
            a_xn(0)
        for idx in range(n):
            b, ti = tiles[idx]
            if ti == 0:
                P.add("dve", "c", lambda e: e.memset(kv_sb, 0.0),
                      writes=bufs("kvsb"), tag=f"kv0_{b}")
                P.add("dve", "c", lambda e: e.memset(ksum_sb, 0.0),
                      writes=bufs("kvsb"), tag=f"ks0_{b}")
            if idx + 2 < n:
                a_load(idx + 2)
            if idx + 1 < n:
                a_xsq(idx + 1)
            a_q(idx)
            if idx + 1 < n:
                a_stats(idx + 1)
            a_kv_mm(idx, 0)
            a_kv_kel(idx, 0)
            a_kv_vm(idx, 0)
            if idx + 1 < n:
                a_bcast(idx + 1)
            a_kv_mm(idx, 1)
            a_kv_upd(idx, 0)
            a_kv_kel(idx, 1)
            a_kv_vm(idx, 1,
                    hook=(lambda idx=idx: a_xn(idx + 1))
                    if idx + 1 < n else None)
            a_kv_upd(idx, 1)
            a_scale(idx)
            if ti == a_tiles - 1:
                a_seqend(b)
        if a_asm and n == len([(b, t) for b in range(BS) for t in range(TPS)]):
            fin_scales()

    def phaseB():
        tern_items = []
        for qt in range(4):
            tern_items.append((lambda qt=qt: tern_prep(
                1024 * qt, f"tfq{qt}", use_prow=True), True))
            for k in (0, 2, 4):
                tern_items.append((lambda qt=qt, k=k:
                                   fc_tern_qt(qt, [k], post_b=False), False))
        tern_sched = {i: fp for i, fp in enumerate(tern_items)}

        def bn_norm(it):
            b, t0 = it // TPS, (it % TPS) * T
            pn, pnB = (p1a, bufs("p1a")) if it % 2 == 0 else (p1b, bufs("p1b"))
            qsl = lambda c, b=b, t0=t0: qs[:, 4096 * c + 2048 * b + t0:
                                           4096 * c + 2048 * b + t0 + T]

            def mm_nrm(e, b=b, qsl=qsl, pn=pn):
                for c in range(KD):
                    i = e.matmul(pn.ap()[0:16, 0:T],
                                 kspad_t.ap()[:, 128 * b + 16 * c:128 * b + 16 * c + 16],
                                 qsl(c), start=(c == 0), stop=(c == KD - 1))
                return i
            P.add("pe", "c", mm_nrm, reads=bufs(f"qst{it}", "kspad"),
                  writes=pnB, tag=f"Bn{it}_nmm")

        def bn_rn(it):
            pn, pnB = (p1a, bufs("p1a")) if it % 2 == 0 else (p1b, bufs("p1b"))
            rb = rnb_ap if it % 2 == 0 else rnb2_ap
            rbB = bufs("rnb" if it % 2 == 0 else "z2")

            def rn_ops(e, pn=pn, rb=rb):
                e.tensor_scalar(rnf_ap, pn.ap()[0:16, 0:T], 1e-6,
                                None, ALU.max)
                with nc.allow_low_precision(reason="normalizer recip; post-LN scale-invariant"):
                    return e.reciprocal(rb, rnf_ap)
            P.add("dve", "c", rn_ops, reads=pnB,
                  writes=bufs("rnf") + rbB, tag=f"Bn{it}_rn")

        def bn_div(it):
            b, t0 = it // TPS, (it % TPS) * T
            tg = f"Bn{it}"
            rb = rnb_ap if it % 2 == 0 else rnb2_ap
            rbB = bufs("rnb" if it % 2 == 0 else "z2")

            def mm_div(e, rb=rb):
                for c in range(KD):
                    if c < 4:
                        dst = p2.ap()[:, 256 * c:256 * c + 256]
                    else:
                        dst = p4.ap()[:, 256 * (c - 4):256 * (c - 4) + 256]
                    i = e.matmul(dst, selm_t.ap()[:, 128 * c:128 * c + 128],
                                 rb, start=(c % 2 == 0), stop=(c % 2 == 1))
                return i
            P.add("pe", "c", mm_div, reads=rbB + bufs("selm"),
                  writes=bufs("p2a", "p2b", "p4a", "p4b"), tag=tg + "_dmm")

            def div_evac(e):
                e.activation(xn[:, 0:1024], p2.ap()[:, 0:1024], AF.Copy)
                return e.activation(xn[:, 1024:2048], p4.ap()[:, 0:1024],
                                    AF.Copy)
            P.add("scalar", "c", div_evac,
                  reads=bufs("p2a", "p2b", "p4a", "p4b"),
                  writes=bufs("xn"), tag=tg + "_dev")

        def bn_qmul(it):
            b, t0 = it // TPS, (it % TPS) * T
            qv = AP(qs.tensor, qs.offset + 2048 * b + t0,
                    [[qs.ap[0][0], 128], [4096, KD], [1, T]])
            P.add("dve", "c", lambda e, qv=qv: e.scalar_tensor_tensor(
                qv, qv, 0.0, xn.rearrange("p (k t) -> p k t", k=KD),
                ALU.bypass, ALU.mult),
                reads=bufs("xn", f"qst{it}"),
                writes=bufs("qs", f"qst{it}"), tag=f"Bn{it}_qmul")

        P.add("dve", "c", lambda e: e.tensor_copy(wfc[0:1, 0:1],
                                                  wfc[0:1, 0:1]),
              writes=bufs("wq", "wkv") +
              [B_[f"fck{k}q{qt}"] for k in range(6)
               for qt in range(4)], tag="fence_wqkv")

        bn_norm(0)
        bn_norm(1)
        bn_rn(0)
        bn_rn(1)
        for it in range(0, NTILE, 2):
            if it + 2 < NTILE:
                bn_norm(it + 2)
                bn_norm(it + 3)
            bn_div(it)
            bn_qmul(it)
            tern_sched[it][0]()
            bn_div(it + 1)
            bn_qmul(it + 1)
            if it + 2 < NTILE:
                bn_rn(it + 2)
                bn_rn(it + 3)
            tern_sched[it + 1][0]()

        zfp = [zf, zf2]
        zfB = [bufs("zf_lo", "zf_hi"), bufs("r2c", "r2d")]
        zfBlo = [bufs("zf_lo"), bufs("r2c")]
        zfBhi = [bufs("zf_hi"), bufs("r2d")]

        def b_ldx(it):
            b, ti = it // TPS, it % TPS
            par = it % 2
            P.add("sync", "d", lambda e, b=b, t0=ti * T, par=par: e.dma_start(
                out=xf[par].rearrange("p (k t) -> p k t", k=KD),
                in_=x_tile(b, t0)), writes=[xfB[par]],
                tag=f"B{b}_{ti}_ldx")

        def b_zevA(it):
            p = it % 2
            tg = f"B{it // TPS}_{it % TPS}"
            P.add("dve", "c", lambda e, p=p: e.tensor_copy(
                zfp[p][:, 0:1024], p2.ap()[:, 0:1024]),
                reads=bufs("p2a", "p2b"), writes=zfBlo[p], tag=tg + "_zevA")

        def b_zevB(it):
            p = it % 2
            tg = f"B{it // TPS}_{it % TPS}"
            P.add("dve", "c", lambda e, p=p: (
                e.tensor_copy(zfp[p][:, 1024:1536], p1a.ap()[:, 0:512]),
                e.tensor_copy(zfp[p][:, 1536:2048], p1b.ap()[:, 0:512]))[-1],
                reads=bufs("p1a", "p1b"), writes=zfBhi[p], tag=tg + "_zevB")

        def b_casts(it):
            p = it % 2
            tg = f"B{it // TPS}_{it % TPS}"

            def casts(e, p=p):
                e.activation(zbf, zfp[p], AF.Copy)
                return e.activation(xn, zfp[p], AF.Square)
            P.add("scalar", "c", casts, reads=zfB[p],
                  writes=bufs("zbf", "xn"), tag=tg + "_casts")

        def b_omm(it):
            b, ti = it // TPS, it % TPS
            t0 = ti * T
            tg = f"B{b}_{ti}"
            qsl = lambda c, b=b, t0=t0: qs[:, 4096 * c + 2048 * b + t0:
                                           4096 * c + 2048 * b + t0 + T]
            qrd = bufs(f"qst{it}", "kvs2") if it < NTILE - 1 else \
                bufs(f"qst{it}", "kvs2", "qs")

            def mm_outA(e, b=b, qsl=qsl):
                for c in range(4):
                    i = e.matmul(p4.ap()[:, 256 * c:256 * c + 256],
                                 kvs2[:, 1024 * b + 128 * c:1024 * b + 128 * c + 128],
                                 qsl(c), start=(c % 2 == 0), stop=(c % 2 == 1))
                return i
            P.add("pe", "c", mm_outA, reads=qrd, writes=p4bufs(0, 1024),
                  tag=tg + "_ommA")

            def mm_outB(e, b=b, qsl=qsl):
                for c in range(4, KD):
                    i = e.matmul(p4.ap()[:, 256 * c:256 * c + 256],
                                 kvs2[:, 1024 * b + 128 * c:1024 * b + 128 * c + 128],
                                 qsl(c), start=(c % 2 == 0), stop=(c % 2 == 1))
                return i
            P.add("pe", "c", mm_outB, reads=qrd, writes=p4bufs(1024, 2048),
                  tag=tg + "_ommB")

        def b_oev(it):
            tg = f"B{it // TPS}_{it % TPS}"
            for s in range(4):
                P.add("scalar", "c", lambda e, s=s: e.activation(
                    attn[:, 512 * s:512 * s + 512],
                    p4.ap()[:, 512 * s:512 * s + 512], AF.Copy),
                    reads=p4bufs(512 * s, 512 * s + 512),
                    writes=bufs("attn"), tag=tg + f"_oev{s}")

        at3 = attn.rearrange("p (k t) -> p k t", k=KD)

        def b_wmmA(it):
            tg = f"B{it // TPS}_{it % TPS}"

            def mm_woA(e):
                for m in range(4):
                    dst = p2.ap()[:, 256 * m:256 * m + 256]
                    for k in range(KD):
                        i = e.matmul(dst,
                                     wo[:, 1024 * k + 128 * m:1024 * k + 128 * m + 128],
                                     at3[:, k, :],
                                     start=(m % 2 == 0 and k == 0),
                                     stop=(m % 2 == 1 and k == KD - 1))
                return i
            P.add("pe", "c", mm_woA, reads=bufs("attn", "wo"),
                  writes=bufs("p2a", "p2b"), tag=tg + "_wmmA")

        def b_wmmB(it):
            tg = f"B{it // TPS}_{it % TPS}"

            def mm_woB(e):
                for m in range(4, KD):
                    if m < 6:
                        dst = p1a.ap()[:, 256 * (m - 4):256 * (m - 4) + 256]
                    else:
                        dst = p1b.ap()[:, 256 * (m - 6):256 * (m - 6) + 256]
                    for k in range(KD):
                        i = e.matmul(dst,
                                     wo[:, 1024 * k + 128 * m:1024 * k + 128 * m + 128],
                                     at3[:, k, :],
                                     start=(m % 2 == 0 and k == 0),
                                     stop=(m % 2 == 1 and k == KD - 1))
                return i
            P.add("pe", "c", mm_woB, reads=bufs("attn", "wo"),
                  writes=bufs("p1a", "p1b"), tag=tg + "_wmmB")

        def b_stats_mm(it):
            tg = f"B{it // TPS}_{it % TPS}"
            pv = pair_view(40960, 24576)

            def mm_stats(e, pv=pv):
                for k in range(KD):
                    i = e.matmul(p4.ap()[0:1, 1536:2048], onesb_t.ap(),
                                 pv(k), start=(k == 0), stop=(k == KD - 1))
                return i
            P.add("pe", "c", mm_stats, reads=bufs("zbf", "xn", "ones"),
                  writes=bufs("p4d"), tag=tg + "_ln2_mm")

            def rows_dve(e):
                r = rows_t.ap()
                e.tensor_scalar(r[0:1, 0:T], p4.ap()[0:1, 1792:1792 + T],
                                1.0 / D, None, ALU.mult)
                e.tensor_scalar(r[0:1, T:2 * T], p4.ap()[0:1, 1536:1536 + T],
                                1.0 / D, None, ALU.mult)
                e.tensor_tensor(r[0:1, 2 * T:3 * T], r[0:1, 0:T], r[0:1, 0:T],
                                ALU.mult)
                return e.tensor_tensor(r[0:1, T:2 * T], r[0:1, T:2 * T],
                                       r[0:1, 2 * T:3 * T], ALU.subtract)
            P.add("dve", "c", rows_dve, reads=bufs("p4d"),
                  writes=bufs("rows"), tag=tg + "_ln2_rows")

        def b_rows_act(it):
            tg = f"B{it // TPS}_{it % TPS}"

            def rows_act(e):
                r = rows_t.ap()
                e.activation(r[0:1, 2 * T:3 * T], r[0:1, T:2 * T], AF.Ln,
                             bias=epsc_t.ap()[0:1, 0:1])
                return e.activation(r[0:1, T:2 * T], r[0:1, 2 * T:3 * T],
                                    AF.Exp, scale=-0.5)
            P.add("scalar", "c", rows_act, reads=bufs("rows", "eps"),
                  writes=bufs("rows"), tag=tg + "_ln2_rstd")

        def b_bc(it):
            tg = f"B{it // TPS}_{it % TPS}"
            P.add("pe", "c", lambda e: e.matmul(
                p4.ap()[:, 1536:2048], onesf_t.ap(),
                rows_t.ap()[0:1, 0:2 * T], start=True, stop=True),
                reads=bufs("rows", "ones"), writes=bufs("p4d"),
                tag=tg + "_ln2_bcast")
            P.add("dve", "c", lambda e: (
                e.tensor_copy(bca, p4.ap()[:, 1536:1536 + T]),
                e.tensor_copy(bcb, p4.ap()[:, 1792:1792 + T]))[-1],
                reads=bufs("p4d"), writes=bufs("bca", "bcb"),
                tag=tg + "_bev")

        def b_fin_half(it, half):
            b, ti = it // TPS, it % TPS
            p = it % 2
            par = it % 2
            tg = f"B{b}_{ti}"
            lo, hi = (0, 4) if half == 0 else (4, KD)
            z3 = zfp[p].rearrange("p (k t) -> p k t", k=KD)[:, lo:hi, :]
            x3 = xf[par].rearrange("p (k t) -> p k t", k=KD)[:, lo:hi, :]
            mbv = bca.unsqueeze(1).broadcast_to([128, hi - lo, T])
            rbv = bcb.unsqueeze(1).broadcast_to([128, hi - lo, T])
            zB = (zfBlo if half == 0 else zfBhi)[p]

            def fin(e, z3=z3, x3=x3, mbv=mbv, rbv=rbv):
                e.scalar_tensor_tensor(z3, z3, 0.0, mbv, ALU.bypass,
                                       ALU.subtract)
                e.scalar_tensor_tensor(z3, z3, 0.0, rbv, ALU.bypass,
                                       ALU.mult)
                return e.scalar_tensor_tensor(x3, z3, 0.0, x3, ALU.bypass,
                                              ALU.add)
            P.add("dve", "c", fin,
                  reads=zB + bufs("bca", "bcb") + [xfB[par]],
                  writes=zB + [xfB[par]], tag=tg + f"_fin{half}")

        def b_stx(it):
            b, ti = it // TPS, it % TPS
            par = it % 2
            P.add("sync", "d", lambda e, b=b, t0=ti * T, par=par:
                  e.dma_start(out=xnew_tile(b, t0),
                              in_=xf[par].rearrange("p (k t) -> p k t", k=KD)),
                  reads=[xfB[par]], writes=[B_[f"xnd{it}"]],
                  tag=f"B{b}_{ti}_stx")

        b_ldx(0)
        b_ldx(1)
        for it in range(NTILE + 3):
            i_fin = it - 3
            i_tail = it - 2
            i_zev = it - 1
            if 0 <= i_fin:
                b_fin_half(i_fin, 0)
            if 0 <= i_zev < NTILE:
                b_zevA(i_zev)
            if 0 <= i_fin:
                b_fin_half(i_fin, 1)
                b_stx(i_fin)
                if i_fin + 2 < NTILE:
                    b_ldx(i_fin + 2)
            if 0 <= i_zev < NTILE:
                b_zevB(i_zev)
            if it < NTILE:
                b_omm(it)
                b_oev(it)
                b_wmmA(it)
            if 0 <= i_tail < NTILE:
                b_stats_mm(i_tail)
            if it < NTILE:
                b_wmmB(it)
            if 0 <= i_tail < NTILE:
                b_rows_act(i_tail)
                b_bc(i_tail)
            if 0 <= i_zev < NTILE:
                b_casts(i_zev)

        P.add("dve", "c", lambda e: e.tensor_copy(wfc[0:1, 0:1],
                                                  wfc[0:1, 0:1]),
              writes=bufs("wo") +
              [B_[f"fck{k}q{qt}"] for k in (6, 7)
               for qt in range(4)], tag="fence_wo")
        P.add("dve", "c", lambda e: e.tensor_copy(qs[0:1, 0:1],
                                                  qs[0:1, 0:1]),
              writes=bufs("qs") +
              [B_[f"pj{k}"] for k in range(32)], tag="fence_qs")

    def fin_scales():
        P.add("sync", "d", lambda e: e.dma_start(out=id_ap, in_=id128_d),
              writes=bufs("r2a"), tag="ld_id")
        spv = scpt_t.ap().rearrange("p (j u) -> p j u", u=4)
        P.add("gpsimd", "c", lambda e: (
            e.tensor_tensor(sccp_t.ap().unsqueeze(2), spv[:, :, 0:1],
                            spv[:, :, 1:2], ALU.add),
            e.tensor_tensor(sccp_t.ap().unsqueeze(2), sccp_t.ap().unsqueeze(2),
                            spv[:, :, 2:3], ALU.add),
            e.tensor_tensor(sccp_t.ap().unsqueeze(2), sccp_t.ap().unsqueeze(2),
                            spv[:, :, 3:4], ALU.add))[-1],
            reads=bufs("scpt", "sccp"), writes=bufs("sccp"), tag="scp_comb")
        P.add("gpsimd", "c", lambda e: e.tensor_scalar(
            sccf_t.ap(), sccf_t.ap(), 1.0 / D, 1e-5, ALU.mult, ALU.max),
            reads=bufs("sccf"), writes=bufs("sccf"), tag="scf_clip")
        P.add("gpsimd", "c", lambda e: e.tensor_scalar(
            sccp_t.ap(), sccp_t.ap(), 1.0 / HID, 1e-5, ALU.mult, ALU.max),
            reads=bufs("sccp"), writes=bufs("sccp"), tag="scp_clip")
        P.add("pe", "c", lambda e: e.matmul(
            p4.ap()[0:32, 1536:1664], sccf_t.ap(), id_ap, start=True, stop=True),
            reads=bufs("sccf", "r2a"), writes=bufs("p4d"), tag="scf_tr")
        P.add("pe", "c", lambda e: e.matmul(
            p4.ap()[0:8, 1792:1920], sccp_t.ap(), id_ap, start=True, stop=True),
            reads=bufs("sccp", "r2a"), writes=bufs("p4d"), tag="scp_tr")
        P.add("dve", "c", lambda e: (
            e.tensor_copy(sct_ap[0:32, :], p4.ap()[0:32, 1536:1664]),
            e.tensor_copy(sct_ap[32:40, :], p4.ap()[0:8, 1792:1920]))[-1],
            reads=bufs("p4d", "r2a"), writes=bufs("sct", "r2a"), tag="sc_ev")
        P.add("scalar", "d", lambda e: e.dma_start(
            out=scal_d[0:1, 0:4096].rearrange("o (a c) -> (o a) c", c=128),
            in_=sct_ap[0:32, :]),
            reads=bufs("sct", "r2a"), writes=bufs("scdram1"), tag="sc_st1")
        P.add("scalar", "d", lambda e: e.dma_start(
            out=scal_d[0:1, 4096:5120].rearrange("o (a c) -> (o a) c", c=128),
            in_=sct_ap[32:40, :]),
            reads=bufs("sct", "r2a"), writes=bufs("scdram2"), tag="sc_st2")

    tern_q = [0]

    scb_sel = [0]

    def tern_prep(scal_off, tag, use_prow=False):
        scb_sel[0] ^= 1
        dst, dstB = ((scb, bufs("r2a", "r2b")) if scb_sel[0] == 0
                     else (scb2, bufs("attn")))
        if use_prow:
            slots = [(prow[0], bufs("prow0"), bufs("prow0", "kvsb")),
                     (prow[1], bufs("prow1"), bufs("prow1", "kvsb"))]
        else:
            srow = sreg.ap()[0:1, 0:2048].bitcast(F32)
            slots = [(rows_t.ap()[0:1, 256:768], bufs("rows"), bufs("rows")),
                     (srow, bufs("z1", "z2", "rnf", "rnb"),
                      bufs("z1", "z2", "rnf", "rnb"))]
        which = "scdram1" if scal_off < 4096 else "scdram2"
        for hh in range(2):
            src, srcB, ldw = slots[hh]
            P.add("scalar", "d", lambda e, o=scal_off + 512 * hh, src=src:
                  e.dma_start(out=src, in_=scal_d[0:1, o:o + 512]),
                  reads=bufs(which), writes=ldw,
                  tag=tag + f"_ld{hh}")
        if use_prow:
            pb, pbB = p4.ap()[:, 1536:2048], bufs("p4d")
        else:
            pb, pbB = p2.ap()[:, 0:512], bufs("p2a")
        for hh in range(2):
            src, srcB, ldw = slots[hh]
            P.add("pe", "c", lambda e, src=src, pb=pb: e.matmul(
                pb, onesf_t.ap(),
                src, start=True, stop=True),
                reads=srcB + bufs("ones"), writes=pbB,
                tag=tag + f"_bc{hh}")
            P.add("dve", "c", lambda e, hh=hh, dst=dst, pb=pb: e.tensor_copy(
                dst[:, 512 * hh:512 * hh + 512], pb),
                reads=pbB,
                writes=dstB, tag=tag + f"_ev{hh}")

    _TPAR = None

    def tern_pair(src_fn, dst_fn, dstbuf_fn, k0, tag, post_b):
        scbv, scbB = ((scb, bufs("r2a", "r2b")) if scb_sel[0] == 0
                      else (scb2, bufs("attn")))
        if post_b:
            pars = [(tchain0, tsg0a, bufs("r2c"), bufs("tsga", "r2d")),
                    (awf, tsg0b, bufs("zbf"), bufs("tsgb", "r2d")),
                    (m1f, zfha, bufs("xn"), bufs("zfha", "zf_hi")),
                    (awf1, zfhb, bufs("zf_lo"), bufs("zfhb", "zf_hi"))]
            npar = 4
        else:
            pars = [(tchain0, tsg0, bufs("r2c"), bufs("r2d")),
                    (awf, awf1, bufs("zbf"), bufs("zf_lo"))]
            npar = 2
        rawB = ([B_["xbf0"]], [B_["xbf1"]])
        units = []
        for u in range(2):
            par = tern_q[0] % npar
            pidx = tern_q[0]
            tern_q[0] += 1
            rawt = rawpair[:, 1024 * u:1024 * u + 1024]
            P.add("sync", "d", lambda e, rawt=rawt, u=u: e.dma_start(
                out=rawt, in_=src_fn(u)),
                writes=rawB[u], tag=tag + f"_ld{u}")
            ch, sg_, chB, sgB = pars[par]
            P.add("scalar", "c", lambda e, rawt=rawt, ch=ch, sg_=sg_: (
                e.activation(ch, rawt, AF.Abs),
                e.activation(sg_, rawt, AF.Sign))[-1],
                reads=rawB[u], writes=list(set(chB + sgB)),
                tag=tag + f"_as{u}")
            units.append((ch, sg_, chB, sgB, pidx))
        (c0, s0, c0B, s0B, p0), (c1, s1, c1B, s1B, p1) = units
        def cmp2(e, c0=c0, c1=c1, scbv=scbv):
            e.scalar_tensor_tensor(c0, c0, 2.0, scbv, ALU.mult, ALU.is_gt)
            e.scalar_tensor_tensor(c1, c1, 2.0, scbv, ALU.mult, ALU.is_gt)
            e.scalar_tensor_tensor(c0, c0, 0.0, scbv, ALU.bypass, ALU.mult)
            return e.scalar_tensor_tensor(c1, c1, 0.0, scbv, ALU.bypass,
                                          ALU.mult)
        P.add("dve", "c", cmp2,
              reads=list(set(c0B + c1B)) + scbB,
              writes=list(set(c0B + c1B)), tag=tag + "_cmp2")
        for u, (ch, sg_, chB, sgB, pidx) in enumerate(units):
            if post_b or pidx % 2 == 0:
                P.add("gpsimd", "c", lambda e, ch=ch, sg_=sg_, k=k0 + u:
                      e.tensor_tensor(dst_fn(k), ch, sg_, ALU.mult),
                      reads=chB + sgB, writes=dstbuf_fn(k0 + u),
                      tag=tag + f"_w{u}")
            else:
                P.add("dve", "c", lambda e, ch=ch, sg_=sg_, k=k0 + u:
                      e.scalar_tensor_tensor(dst_fn(k), ch, 0.0, sg_,
                                             ALU.bypass, ALU.mult),
                      reads=chB + sgB, writes=dstbuf_fn(k0 + u),
                      tag=tag + f"_w{u}")

    def fc_dst(k_qt):
        k, qt = k_qt
        return wfc[:, 4096 * k + 1024 * qt: 4096 * k + 1024 * qt + 1024]

    def fc_dstbuf(k_qt):
        k, qt = k_qt
        return [B_[f"fck{k}q{qt}"]]

    def fc_src(qt, k):
        return lambda u: wfct_d.rearrange("(k p) h -> p k h", p=128)[:, k + u, 1024 * qt:1024 * qt + 1024]

    def fc_tern_qt(qt, ks, post_b):
        for k in ks:
            def dst(kk, qt=qt):
                return fc_dst((kk, qt))
            def dbf(kk, qt=qt):
                return fc_dstbuf((kk, qt))
            tern_pair(fc_src(qt, k), dst, dbf, k, f"tf{qt}_{k}", post_b)

    def pj_tern(post_b=True):
        for k2 in range(0, 32, 2):
            tern_pair(
                lambda u, k2=k2: wpjt_d.rearrange("(k p) o -> p k o", p=128)[:, k2 + u, :],
                lambda k: wpj[:, 1024 * k:1024 * k + 1024],
                lambda k: [B_[f"pj{k}"]], k2, f"tp{k2}", post_b)

    def phaseC():
        P.add("pe", "c", lambda e: e.matmul(
            p1a.ap()[0:16, 0:1], onesf_t.ap()[0:1, 0:16], gates_t.ap(),
            start=True, stop=True),
            reads=bufs("gates", "ones"), writes=bufs("p1a"), tag="gate_mm")
        P.add("dve", "c", lambda e: e.tensor_copy(gatev_t.ap(), p1a.ap()[0:16, 0:1]),
              reads=bufs("p1a"), writes=bufs("gatev"), tag="gate_ev")

        def mm_vones(e):
            for k in range(KD):
                i = e.matmul(p1b.ap()[0:16, 0:1],
                             vt_t.ap()[:, 16 * k:16 * k + 16],
                             onesb_t.ap(),
                             start=(k == 0), stop=(k == KD - 1))
            return i
        P.add("pe", "c", mm_vones, reads=bufs("vt", "ones"),
              writes=bufs("p1b"), tag="vones_mm")
        P.add("dve", "c", lambda e: e.tensor_copy(vones_t.ap(),
                                                  p1b.ap()[0:16, 0:1]),
              reads=bufs("p1b"), writes=bufs("vones"), tag="vones_ev")
        P.add("gpsimd", "d", lambda e: e.dma_start(out=ut, in_=ut_d),
              reads=[], writes=bufs("kvsb", "kvs2", "prow0", "prow1"),
              tag="ld_ut")

        tiles = [(b, ti) for b in range(BS) for ti in range(TPS)]

        def c_ldx(idx):
            b, ti = tiles[idx]
            t0 = ti * T
            par = idx % 2
            P.add("sync", "d", lambda e, b=b, t0=t0, par=par: e.dma_start(
                out=xf[par].rearrange("p (k t) -> p k t", k=KD),
                in_=xnew_tile(b, t0)),
                reads=[B_[f"xnd{idx}"]], writes=[xfB[par]],
                tag=f"C{b}_{ti}_ldx")

        c_ldx(0)
        c_ldx(1)

        for qt in range(4):
            tern_prep(1024 * qt, f"tfp{qt}")
            fc_tern_qt(qt, [6], post_b=True)
        tern_prep(4096, "tpp")
        pj_tern(post_b=True)

        def c_ldxb(idx):
            b, ti = tiles[idx]
            t0 = ti * T
            par = idx % 2
            P.add("gpsimd", "d", lambda e, b=b, t0=t0, par=par: e.dma_start(
                out=xbf[par].rearrange("p (k t) -> p k t", k=KD),
                in_=xnew_tile(b, t0)),
                reads=[B_[f"xnd{idx}"]], writes=[xbfB[par]],
                tag=f"C{b}_{ti}_ldxb")

        def c_xsq(idx):
            b, ti = tiles[idx]
            par = idx % 2
            P.add("scalar", "c", lambda e, par=par: e.activation(
                attn, xbf[par], AF.Square),
                reads=[xbfB[par]], writes=bufs("attn"), tag=f"C{b}_{ti}_xsq")

        def c_stats(idx):
            b, ti = tiles[idx]
            par = idx % 2
            tg = f"C{b}_{ti}"
            lnorm_stats(pair_view(16384 + 4096 * par, 28672),
                        [xbfB[par], B_["attn"]], tg + "_ln3", split_bcast=True)

        def c_bcast(idx):
            b, ti = tiles[idx]
            lnorm_bcast(f"C{b}_{ti}_ln3")

        def c_xn(idx):
            b, ti = tiles[idx]
            par = idx % 2
            ln_apply(xf[par], xn, [xfB[par]], bufs("xn"), f"C{b}_{ti}_xn2")

        def c_vx(idx):
            b, ti = tiles[idx]
            par = idx % 2
            tg = f"C{b}_{ti}"
            xb3 = xbf[par].rearrange("p (k t) -> p k t", k=KD)

            def mm_vx(e, xb3=xb3):
                for k in range(KD):
                    i = e.matmul(p1b.ap()[0:16, 0:T],
                                 vt_t.ap()[:, 16 * k:16 * k + 16],
                                 xb3[:, k, :],
                                 start=(k == 0), stop=(k == KD - 1))
                return i
            P.add("pe", "c", mm_vx, reads=[xbfB[par]] + bufs("vt"),
                  writes=bufs("p1b"), tag=tg + "_vx")
            P.add("dve", "c", lambda e: e.tensor_copy(vxs, p1b.ap()[0:16, 0:T]),
                  reads=bufs("p1b"), writes=bufs("rnf"), tag=tg + "_vxe")

        def c_zcorr(idx):
            b, ti = tiles[idx]
            tg = f"C{b}_{ti}"
            vtmp = kspad_t.ap()[0:16, 0:T]

            def corr(e, vtmp=vtmp):
                e.tensor_scalar(vtmp, p1a.ap()[0:16, 0:T],
                                vones_t.ap()[0:16, 0:1], None, ALU.mult)
                e.scalar_tensor_tensor(vxs, vxs, 0.0, vtmp, ALU.bypass,
                                       ALU.subtract)
                return e.scalar_tensor_tensor(z1_ap, vxs, 0.0,
                                              p1a.ap()[0:16, T:2 * T],
                                              ALU.bypass, ALU.mult)
            P.add("dve", "c", corr,
                  reads=bufs("p1a", "rnf", "vones", "kspad"),
                  writes=bufs("z1", "kspad", "rnf"), tag=tg + "_zc")

        def c_z2(idx):
            b, ti = tiles[idx]
            tg = f"C{b}_{ti}"
            P.add("pe", "c", lambda e: e.matmul(
                p1b.ap()[0:16, 0:T], fbt_t.ap(), z1_ap,
                start=True, stop=True),
                reads=bufs("z1", "fbt"), writes=bufs("p1b"), tag=tg + "_z2")
            P.add("dve", "c", lambda e: e.tensor_scalar(
                z2_ap, p1b.ap()[0:16, 0:T], gatev_t.ap()[0:16, 0:1],
                None, ALU.mult),
                reads=bufs("p1b", "gatev"), writes=bufs("z2"), tag=tg + "_z2e")

        def c_fc(idx, g):
            b, ti = tiles[idx]
            tg = f"C{b}_{ti}"
            xn3 = xn.rearrange("p (k t) -> p k t", k=KD)
            bank = g % 2
            pdst = p2.ap()[:, 512 * bank:512 * bank + 512]
            pbu = [B_["p2a"] if bank == 0 else B_["p2b"]]
            rl_dst = zf_lo[:, 0:512] if bank == 0 else zf_hi[:, 0:512]
            rlB = [B_["zf_lo"] if bank == 0 else B_["zf_hi"]]

            def mm_fc(e, g=g, pdst=pdst, xn3=xn3):
                for mi in range(2):
                    e.matmul(pdst[:, 256 * mi:256 * mi + 256],
                             ut[:, 128 * (2 * g + mi):128 * (2 * g + mi) + 128],
                             z2_ap, start=(mi == 0), stop=False)
                for mi in range(2):
                    m = 2 * g + mi
                    for k in range(KD):
                        i = e.matmul(pdst[:, 256 * mi:256 * mi + 256],
                                     wfc[:, 4096 * k + 128 * m:4096 * k + 128 * m + 128],
                                     xn3[:, k, :],
                                     start=False,
                                     stop=(mi == 1 and k == KD - 1))
                return i
            P.add("pe", "c", mm_fc,
                  reads=bufs("xn", "z2", "kvsb", "kvs2") +
                  [B_[f"fck{k}q{g // 4}"] for k in range(KD)],
                  writes=pbu, tag=tg + f"_fc{g}")
            P.add("scalar", "c", lambda e, pdst=pdst, rl_dst=rl_dst:
                  e.activation(rl_dst, pdst, AF.Relu),
                  reads=pbu, writes=rlB, tag=tg + f"_rl{g}")
            P.add("scalar", "c", lambda e, g=g, rl_dst=rl_dst: e.activation(
                r2s[:, 512 * g:512 * g + 512], rl_dst, AF.Square),
                reads=rlB, writes=[B_["r2" + "abcd"[g // 4]]], tag=tg + f"_sq{g}")

        def c_proj(idx):
            b, ti = tiles[idx]
            tg = f"C{b}_{ti}"

            def mm_pj(e):
                for m2 in range(KD):
                    for k2 in range(32):
                        i = e.matmul(p4.ap()[:, 256 * m2:256 * m2 + 256],
                                     wpj[:, 1024 * k2 + 128 * m2:1024 * k2 + 128 * m2 + 128],
                                     r2s[:, 256 * k2:256 * k2 + 256],
                                     start=(k2 == 0), stop=(k2 == 31))
                return i
            P.add("pe", "c", mm_pj,
                  reads=bufs("r2a", "r2b", "r2c", "r2d", "qs") +
                  [B_[f"pj{k}"] for k in range(32)],
                  writes=p4bufs(0, 2048), tag=tg + "_pj")

        def c_tail_ev(idx):
            b, ti = tiles[idx]
            tg = f"C{b}_{ti}"
            P.add("scalar", "c", lambda e: e.activation(zbf, p4.ap()[:, 0:2048], AF.Copy),
                  reads=p4bufs(0, 2048), writes=bufs("zbf"), tag=tg + "_obf")
            P.add("scalar", "c", lambda e: e.activation(attn, p4.ap()[:, 0:2048], AF.Square),
                  reads=p4bufs(0, 2048), writes=bufs("attn"), tag=tg + "_osq")

        def c_tail_stats(idx):
            b, ti = tiles[idx]
            lnorm_stats(pair_view(40960, 28672),
                        bufs("zbf", "attn"), f"C{b}_{ti}_ln4", x_first=False,
                        split_bcast=True)

        def c_tail_fin(idx):
            b, ti = tiles[idx]
            t0 = ti * T
            par = idx % 2
            tg = f"C{b}_{ti}"
            lnorm_bcast(tg + "_ln4")
            P.add("dve", "c", lambda e: (
                e.tensor_copy(bca, p1a.ap()[:, 0:T]),
                e.tensor_copy(bcb, p1a.ap()[:, T:2 * T]))[-1],
                reads=bufs("p1a"), writes=bufs("bca", "bcb"),
                tag=tg + "_bev")
            z3 = zf.rearrange("p (k t) -> p k t", k=KD)
            x3 = xf[par].rearrange("p (k t) -> p k t", k=KD)
            p43 = p4.ap()[:, 0:2048].rearrange("p (k t) -> p k t", k=KD)
            mbv = bca.unsqueeze(1).broadcast_to([128, KD, T])
            rbv = bcb.unsqueeze(1).broadcast_to([128, KD, T])

            def fin(e, z3=z3, x3=x3, p43=p43, mbv=mbv, rbv=rbv):
                e.scalar_tensor_tensor(z3, p43, 0.0, mbv, ALU.bypass,
                                       ALU.subtract)
                e.scalar_tensor_tensor(z3, z3, 0.0, rbv, ALU.bypass,
                                       ALU.mult)
                return e.scalar_tensor_tensor(x3, z3, 0.0, x3, ALU.bypass,
                                              ALU.add)
            P.add("dve", "c", fin,
                  reads=p4bufs(0, 2048) + bufs("bca", "bcb") + [xfB[par]],
                  writes=bufs("zf_lo", "zf_hi") + [xfB[par]], tag=tg + "_fin")
            P.add("gpsimd", "d", lambda e, b=b, t0=t0, par=par: e.dma_start(
                out=out_tile(b, t0),
                in_=xf[par].rearrange("p (k t) -> p k t", k=KD)),
                reads=[xfB[par]] + bufs("odram"), tag=tg + "_sto")

        n = len(tiles)
        c_ldxb(0)
        if n > 1:
            c_ldxb(1)
        c_xsq(0)
        c_stats(0)
        c_vx(0)
        c_bcast(0)
        c_zcorr(0)
        c_xn(0)
        c_z2(0)
        for idx in range(n):
            if idx + 2 < n:
                c_ldxb(idx + 2)
            if idx > 0:
                c_tail_ev(idx - 1)
            for g in range(4):
                c_fc(idx, g)
            if idx > 0:
                c_tail_stats(idx - 1)
            for g in range(4, 7):
                c_fc(idx, g)
            if idx > 0:
                c_tail_fin(idx - 1)
            if idx + 1 < n:
                c_xsq(idx + 1)
            if idx + 1 < n and idx + 1 > 1:
                c_ldx(idx + 1)
            for g in range(7, 11):
                c_fc(idx, g)
            if idx + 1 < n:
                c_stats(idx + 1)
                c_vx(idx + 1)
            for g in range(11, 13):
                c_fc(idx, g)
            if idx + 1 < n:
                c_bcast(idx + 1)
                c_zcorr(idx + 1)
            for g in range(13, 16):
                c_fc(idx, g)
            if idx + 1 < n:
                c_xn(idx + 1)
                c_z2(idx + 1)
            c_proj(idx)
        c_tail_ev(n - 1)
        c_tail_stats(n - 1)
        c_tail_fin(n - 1)

    ph0()
    if "A" in phases:
        phaseA()
    if "B" in phases:
        phaseB()
    if "C" in phases:
        phaseC()

    if debug_taps:
        P.add("scalar", "d", lambda e: e.dma_start(out=dq_d, in_=qs),
              reads=bufs("qs", "odram"), tag="dbg_q")
        P.add("scalar", "d", lambda e: e.dma_start(out=dkv_d, in_=kvs2),
              reads=bufs("kvs2", "odram"), tag="dbg_kv")
        P.add("scalar", "d", lambda e: e.dma_start(out=dks_d, in_=kspad_t.ap()),
              reads=bufs("kspad", "odram"), tag="dbg_ks")
        if "C" in phases:
            P.add("scalar", "d", lambda e: e.dma_start(out=dwf_d, in_=wfc),
                  reads=bufs("wq", "wkv", "wo", "odram") +
                  [B_[f"fck{k}q{qt}"] for k in range(8) for qt in range(4)],
                  tag="dbg_wfc")
            P.add("scalar", "d", lambda e: e.dma_start(out=dwp_d, in_=wpj),
                  reads=bufs("qs", "odram") +
                  [B_[f"pj{k}"] for k in range(32)], tag="dbg_wpj")
            P.add("scalar", "d", lambda e: e.dma_start(out=dsc_d, in_=scal_d),
                  reads=bufs("scdram1", "scdram2", "odram"), tag="dbg_scal")

    P.drain_op = P.add("sync", "c", lambda e: None,
                       writes=bufs("odram"), tag="final_wait")

    from contextlib import ExitStack
    with ExitStack() as stack:
        sems = {}
        for s in ("pe", "dve", "act", "pool", "syncc"):
            sems[(s, 0)] = stack.enter_context(nc.semaphore(f"s_{s}"))
        for q in ("sync", "gdma", "sdma"):
            for j in range(Prog.NSLOT):
                sems[(q, j)] = stack.enter_context(nc.semaphore(f"s_{q}{j}"))
        block = stack.enter_context(nc.Block())
        bodies = P.emit(nc, sems)
        block.tensor(bodies["pe"])
        block.vector(bodies["dve"])
        block.scalar(bodies["scalar"])
        block.gpsimd(bodies["gpsimd"])
        block.sync(bodies["sync"])

    return nc



def _host_inputs(L0_attn_x, L0_attn_w_qkv, L0_attn_w_out, L0_ffn_fast_b,
                 L0_ffn_w_fc, L0_ffn_w_proj, L0_ffn_u, L0_ffn_v, L0_ffn_gate):
    x = np.asarray(L0_attn_x, dtype=np.float32)
    wqkv = np.asarray(L0_attn_w_qkv, dtype=np.float32)
    shared = {
        "wq": np.ascontiguousarray(wqkv[0:D].T),
        "wkv": np.ascontiguousarray(wqkv[D:3 * D].T),
        "wo": np.ascontiguousarray(np.asarray(L0_attn_w_out, np.float32).T),
        "wfcn": np.ascontiguousarray(np.asarray(L0_ffn_w_fc, np.float32)),
        "wfct": np.ascontiguousarray(np.asarray(L0_ffn_w_fc, np.float32).T),
        "wpjn": np.ascontiguousarray(np.asarray(L0_ffn_w_proj, np.float32)),
        "wpjt": np.ascontiguousarray(np.asarray(L0_ffn_w_proj, np.float32).T),
        "vt": np.ascontiguousarray(np.asarray(L0_ffn_v, np.float32).T),
        "fbt": np.ascontiguousarray(np.asarray(L0_ffn_fast_b, np.float32).T),
        "ut": np.ascontiguousarray(np.asarray(L0_ffn_u, np.float32).T),
        "gate": np.asarray(L0_ffn_gate, np.float32).reshape(1, 1),
        "id128": np.eye(128, dtype=np.float32),
    }
    selm = np.zeros((H, 1024), np.float32)
    for c in range(KD):
        selm[2 * c, 128 * c:128 * c + 64] = 1.0
        selm[2 * c + 1, 128 * c + 64:128 * c + 128] = 1.0
    shared["selm"] = selm
    in_maps = []
    for core in range(NCORES):
        m = dict(shared)
        m["x"] = np.ascontiguousarray(
            x[BS * core:BS * (core + 1)].transpose(0, 2, 1))
        in_maps.append(m)
    return in_maps


def kernel(**inputs):
    in_maps = _host_inputs(**inputs)
    nc = build_program()
    res = run_bass_kernel_spmd(nc, in_maps, core_ids=list(range(NCORES)))
    parts = [r["out"].transpose(0, 2, 1) for r in res.results]
    return np.ascontiguousarray(np.concatenate(parts, axis=0)).astype(np.float32)
